# version 01
# baseline (speedup 1.0000x reference)
import math
import numpy as np
import concourse.bass as bass
import concourse.mybir as mybir
from concourse.bass_utils import run_bass_kernel_spmd

F32 = mybir.dt.float32
BF16 = mybir.dt.bfloat16
AF = mybir.ActivationFunctionType
ALU = mybir.AluOpType
AX = mybir.AxisListType

NCORES = 8
T = 1024
TH = 512
C = 128
NCH = 8
D = 2048
DC = 16
HID = 1024
HC = 8
NE = 9
NH = 32
NKH = 16
DK = 128
DV = 128
QKVZ_OC = 96
MIX_NC = 64
EPS_LN = 1e-5
EPS_L2 = 1e-6
EPS_RMS = 1e-6
BIG = 1e30
NLEV = 7
HGRP = 8
NGRP = NH // HGRP


def _f32c(x):
    return np.ascontiguousarray(x, dtype=np.float32)


def host_prep(inputs):
    x = np.asarray(inputs["L0_attn_hidden_states"])[0]
    w_qkvz = np.asarray(inputs["L0_attn_w_qkvz"])
    w_ba = np.asarray(inputs["L0_attn_w_ba"])
    conv_w = np.asarray(inputs["L0_attn_conv_w"])
    dt_bias = np.asarray(inputs["L0_attn_dt_bias"])
    A_log = np.asarray(inputs["L0_attn_A_log"])
    norm_w = np.asarray(inputs["L0_attn_norm_w"])
    w_out = np.asarray(inputs["L0_attn_w_out"])
    gate = np.asarray(inputs["L0_ffn_gate"])
    ebias = np.asarray(inputs["L0_ffn_expert_bias"])
    w1 = np.asarray(inputs["L0_ffn_w1"])
    w2 = np.asarray(inputs["L0_ffn_w2"])
    w3 = np.asarray(inputs["L0_ffn_w3"])
    sw1 = np.asarray(inputs["L0_ffn_shared_w1"])
    sw2 = np.asarray(inputs["L0_ffn_shared_w2"])
    sw3 = np.asarray(inputs["L0_ffn_shared_w3"])

    sh = {}
    wq = w_qkvz.reshape(DC, 128, QKVZ_OC, 128)
    sh["wq_t"] = _f32c(wq.transpose(2, 1, 0, 3))
    sh["wba_t"] = _f32c(w_ba.reshape(DC, 128, 64).transpose(1, 0, 2))
    sh["gate_t"] = _f32c(gate.T.reshape(DC, 128, 8).transpose(1, 0, 2))
    w13 = []
    for warr, swarr in ((w1, sw1), (w3, sw3)):
        wa = np.concatenate([warr, swarr[None]], 0)
        wt = wa.reshape(NE, HC, 128, DC, 128)
        w13.append(_f32c(wt.transpose(0, 1, 4, 3, 2)))
    sh["w1_t"], sh["w3_t"] = w13
    w2a = np.concatenate([w2, sw2[None]], 0)
    w2t = w2a.reshape(NE, DC, 128, HC, 128)
    sh["w2_t"] = _f32c(w2t.transpose(0, 1, 4, 3, 2))
    wo = w_out.reshape(NH, 128, DC, 128)
    sh["wout_t"] = _f32c(wo.transpose(1, 0, 2, 3))
    cw = conv_w[:, 0, :].reshape(MIX_NC, 128, 4)
    idx = np.arange(128)
    diag = np.zeros((MIX_NC, 4, 128, 128), np.float32)
    for mc in range(MIX_NC):
        for j in range(4):
            diag[mc, j, idx, idx] = cw[mc, :, j]
    sh["conv_diag"] = _f32c(diag.transpose(0, 2, 1, 3))
    sh["conv_pp"] = _f32c(cw.transpose(1, 0, 2))
    sh["A_col"] = _f32c(A_log.reshape(32, 1))
    sh["dt_col"] = _f32c(dt_bias.reshape(32, 1))
    sh["normw_col"] = _f32c(norm_w.reshape(128, 1))
    sh["ebias_row"] = _f32c(ebias.reshape(1, 8))

    cn = {}
    cn["identf"] = _f32c(np.eye(128))
    cn["onesf"] = _f32c(np.ones((128, 128)))
    cn["negmaskU_T"] = _f32c(-BIG * (idx[:, None] <= idx[None, :]))
    cn["negmaskI_T"] = _f32c(-BIG * (idx[:, None] < idx[None, :]))
    cn["negmaskL_T"] = _f32c(-BIG * (idx[None, :] <= idx[:, None]))
    E = np.zeros((NH, 128, 128), np.float32)
    E[np.arange(NH), :, np.arange(NH)] = 1.0
    cn["E_col"] = E
    Er = np.zeros((NH, 128, 128), np.float32)
    Er[np.arange(NH), np.arange(NH), :] = 1.0
    cn["E_row"] = Er
    E8 = np.zeros((8, 8, 128), np.float32)
    E8[np.arange(8), np.arange(8), :] = 1.0
    cn["E8_row"] = E8
    ccols = np.zeros((128, 8), np.float32)
    ccols[:, 0] = EPS_LN; ccols[:, 1] = 1.0; ccols[:, 2] = 128 * EPS_L2
    ccols[:, 3] = EPS_L2; ccols[:, 4] = EPS_RMS
    cn["ccols"] = ccols

    per_core = []
    for c in range(NCORES):
        d = {}
        d["x_fm"] = _f32c(x[c * T:(c + 1) * T].T)
        oh = np.zeros((1, 8), np.float32); oh[0, c] = 1.0
        d["onehot"] = oh
        ohp = np.zeros((1, 8), np.float32)
        if c > 0:
            ohp[0, c - 1] = 1.0
        d["onehot_prev"] = ohp
        per_core.append(d)
    return sh, cn, per_core


class Prog:
    ENGINES = ("sync", "scalar", "vector", "tensor", "gpsimd")

    def __init__(self, nc):
        self.nc = nc
        self.q = {e: [] for e in self.ENGINES}
        self._sems = {}
        self.cnt = {}
        self._ctx = []
        self.last = {}
        self.pending = {}
        self.lastsig = {}

    def sem(self, name):
        if name not in self._sems:
            cm = self.nc.semaphore(name)
            self._sems[name] = cm.__enter__()
            self._ctx.append(cm)
            self.cnt[name] = 0
        return self._sems[name]

    def bump(self, name, n=1):
        s = self.sem(name)
        self.cnt[name] += n
        return s, self.cnt[name]

    def on(self, eng, fn):
        self.q[eng].append(fn)

    def dma(self, qeng, out, in_, sem_name):
        s, v = self.bump(sem_name, 16)

        def f(e):
            e.dma_start(out=out, in_=in_).then_inc(s, 16)
        self.on(qeng, f)
        self.pending[qeng] = "dma"
        return v

    def wait(self, eng, sem_name, val):
        s = self.sem(sem_name)
        self.on(eng, lambda e: e.wait_ge(s, val))

    def wait_now(self, eng, sem_name):
        self.wait(eng, sem_name, self.cnt[sem_name])

    PRIM = {"tensor": "tC", "vector": "vC", "scalar": "sC", "gpsimd": "gC"}

    def run(self, eng, fn):
        def f(e):
            inst = fn(e)
            if inst is not None:
                self.last[eng] = inst
        self.on(eng, f)
        self.pending[eng] = "inst"

    def run_sig(self, eng, fn, sem_name, n=1):
        s, v = self.bump(sem_name, n)

        def f(e):
            inst = fn(e)
            inst.then_inc(s, n)
            self.last[eng] = inst
        self.on(eng, f)
        self.pending[eng] = None
        self.lastsig[eng] = (sem_name, v)
        return v

    def sig(self, eng, sem_name, n=1):
        assert self.pending.get(eng) == "inst", f"sig on {eng} without pending instruction"
        s, v = self.bump(sem_name, n)

        def f(e):
            self.last[eng].then_inc(s, n)
        self.on(eng, f)
        self.pending[eng] = None
        self.lastsig[eng] = (sem_name, v)
        return v

    def token(self, eng):
        if self.pending.get(eng) == "inst":
            v = self.sig(eng, self.PRIM[eng])
            return (self.PRIM[eng], v)
        return self.lastsig.get(eng)

    def barrier(self):
        dma_sems = [(nm, self.cnt[nm]) for nm in list(self.cnt)
                    if nm.startswith("d") and self.cnt[nm] > 0]
        toks = []
        for eng in ("tensor", "vector", "scalar"):
            t = self.token(eng)
            if t is not None:
                toks.append(t)
        s = self.sem("bar")
        self.cnt["bar"] += 2
        tgt = self.cnt["bar"]
        for eng in ("sync", "gpsimd"):
            self.on(eng, lambda e, s=s: e.nop().then_inc(s, 1))
        for eng in self.ENGINES:
            for nm, val in dma_sems + toks:
                self.wait(eng, nm, val)
            self.on(eng, lambda e, s=s, tgt=tgt: e.wait_ge(s, tgt))

    def emit(self, block):
        decs = {"sync": block.sync, "scalar": block.scalar,
                "vector": block.vector, "tensor": block.tensor,
                "gpsimd": block.gpsimd}
        for eng in self.ENGINES:
            fns = self.q[eng]
            if not fns:
                continue

            def body(e, fns=fns):
                for f in fns:
                    f(e)
            decs[eng](body)


class Kern:

    def __init__(self, nc, dbg=()):
        self.nc = nc
        self.p = Prog(nc)
        self.dbg_names = set(dbg)
        self.dbg_outs = {}
        self._cm = []
        self.pe_mode = "mm"
        self.declare()

    def al(self, kind, name, shape, dt):
        f = self.nc.sbuf_tensor if kind == "s" else self.nc.psum_tensor
        cm = f(name, shape, dt)
        h = cm.__enter__()
        self._cm.append(cm)
        return h

    def close(self):
        if self._x_cms is not None:
            self.free_x()
            self._x_cms = None
        for cm in reversed(self._cm):
            cm.__exit__(None, None, None)
        for cm in reversed(self.p._ctx):
            cm.__exit__(None, None, None)

    def declare(self):
        nc = self.nc
        dram = lambda n, s, d, k=None: (nc.dram_tensor(n, s, d, kind=k) if k
                                        else nc.dram_tensor(n, s, d)).ap()
        DI = lambda n, s, d=F32: dram(n, s, d, "ExternalInput")
        DO = lambda n, s, d=F32: dram(n, s, d, "ExternalOutput")
        DS = lambda n, s, d=F32: dram(n, s, d)
        self.d = d = {}
        d["x_fm"] = DI("x_fm", [D, T])
        d["wq_t"] = DI("wq_t", [QKVZ_OC, 128, DC, 128])
        d["wba_t"] = DI("wba_t", [128, DC, 64])
        d["gate_t"] = DI("gate_t", [128, DC, 8])
        d["w1_t"] = DI("w1_t", [NE, HC, 128, DC, 128])
        d["w3_t"] = DI("w3_t", [NE, HC, 128, DC, 128])
        d["w2_t"] = DI("w2_t", [NE, DC, 128, HC, 128])
        d["wout_t"] = DI("wout_t", [128, NH, DC, 128])
        d["conv_diag"] = DI("conv_diag", [MIX_NC, 128, 4, 128])
        d["conv_pp"] = DI("conv_pp", [128, MIX_NC, 4])
        d["A_col"] = DI("A_col", [32, 1])
        d["dt_col"] = DI("dt_col", [32, 1])
        d["normw_col"] = DI("normw_col", [128, 1])
        d["ebias_row"] = DI("ebias_row", [1, 8])
        d["onehot"] = DI("onehot", [1, 8])
        d["onehot_prev"] = DI("onehot_prev", [1, 8])
        d["identf"] = DI("identf", [128, 128])
        d["onesf"] = DI("onesf", [128, 128])
        d["identb"] = DI("identb", [128, 128], BF16)
        d["negidentb"] = DI("negidentb", [128, 128], BF16)
        d["onesb"] = DI("onesb", [128, 128], BF16)
        d["negmaskU_T"] = DI("negmaskU_T", [128, 128])
        d["negmaskI_T"] = DI("negmaskI_T", [128, 128])
        d["negmaskL_T"] = DI("negmaskL_T", [128, 128])
        d["E_col"] = DI("E_col", [NH, 128, 128], BF16)
        d["E_row"] = DI("E_row", [NH, 128, 128], BF16)
        d["E8_row"] = DI("E8_row", [8, 8, 128])
        d["ccols"] = DI("ccols", [128, 8])
        d["out_fm"] = DO("out_fm", [D, T])
        d["x2_sp"] = DS("x2_sp", [DC, 128, T])
        d["z_sp"] = DS("z_sp", [NCH, NH, 128, C], BF16)
        d["q_sp"] = DS("q_sp", [NKH, 128, T], BF16)
        d["ktm_sp"] = DS("ktm_sp", [NCH, NKH, 128, 128], BF16)
        d["mt_sp"] = DS("mt_sp", [NCH, NH, 128, 128], BF16)
        d["ua_sp"] = DS("ua_sp", [NCH, NH, 128, 128], BF16)
        d["mw_sp"] = DS("mw_sp", [NCH, NH, 128, 128], BF16)
        d["wbc_sp"] = DS("wbc_sp", [NCH, NH, 128, 128], BF16)
        d["lnb_sp"] = DS("lnb_sp", [DC, 128, T], BF16)
        d["v_sp"] = DS("v_sp", [NH, 128, T], BF16)
        d["y_sp"] = DS("y_sp", [NH, 128, T], BF16)
        d["ag1_in"] = DS("ag1_in", [MIX_NC, 128, 3])
        d["ag1_out"] = DS("ag1_out", [NCORES, MIX_NC, 128, 3])
        d["ag2_in"] = DS("ag2_in", [2, NH, 128, 128])
        d["ag2_out"] = DS("ag2_out", [NCORES, 2, NH, 128, 128])

        self.c_identf = self.al("s", "c_identf", [128, 128], F32)
        self.c_onesf = self.al("s", "c_onesf", [128, 128], F32)
        self.c_identb = self.al("s", "c_identb", [128, 128], BF16)
        self.c_negidentb = self.al("s", "c_negidentb", [128, 128], BF16)
        self.c_onesb = self.al("s", "c_onesb", [128, 128], BF16)
        self.c_mU = self.al("s", "c_mU", [128, 128], F32)
        self.c_mI = self.al("s", "c_mI", [128, 128], F32)
        self.c_mL = self.al("s", "c_mL", [128, 128], F32)
        self.c_E8 = self.al("s", "c_E8", [8, 8, 128], F32)
        self.c_cols = self.al("s", "c_cols", [128, 8], F32)
        self.scr0 = self.al("s", "scr0", [128, 16], F32)
        self.junk = self.al("s", "junk", [128, 4], F32)
        self.c_Acol = self.al("s", "c_Acol", [32, 1], F32)
        self.c_negexpA = self.al("s", "c_negexpA", [32, 1], F32)
        self.c_dtcol = self.al("s", "c_dtcol", [32, 1], F32)
        self.c_normw = self.al("s", "c_normw", [128, 1], F32)
        self.c_ebias_b = self.al("s", "c_ebias_b", [128, 8], F32)
        self.c_oh = self.al("s", "c_oh", [128, 8], F32)
        self.c_ohp = self.al("s", "c_ohp", [128, 8], F32)
        self.gate_sb = self.al("s", "gate_sb", [128, DC, 8], BF16)
        self.ebias_row_sb = self.al("s", "ebias_row_sb", [1, 8], F32)
        self.oh_row_sb = self.al("s", "oh_row_sb", [1, 8], F32)
        self.ohp_row_sb = self.al("s", "ohp_row_sb", [1, 8], F32)
        self.lnb = self.al("s", "lnb", [128, DC, T], BF16)
        self.X = None
        self._x_cms = None
        self.psA = self.al("p", "psA", [128, 8, 512], F32)

    def bank(self, i, lo=0, n=512):
        return self.psA[:, i, lo:lo + n]

    def alloc_x(self):
        cms = []

        def al2(name, shape, dt):
            cm = self.nc.sbuf_tensor(name, shape, dt)
            h = cm.__enter__()
            cms.append(cm)
            return h
        n = 0 if self._x_cms is None else 1
        self.ln_xb = al2(f"ln_xb{n}", [128, 2, TH], BF16)
        self.ln_sq = al2(f"ln_sq{n}", [128, 2, TH], BF16)
        self.ln_mean = al2(f"ln_mean{n}", [128, TH], F32)
        self.ln_var = al2(f"ln_var{n}", [128, TH], F32)
        self.ln_t = al2(f"ln_t{n}", [128, 2, TH], F32)
        self.X = al2(f"X{n}", [128, DC, T], F32)
        self._x_cms = cms

    def free_x(self):
        for cm in reversed(self._x_cms):
            cm.__exit__(None, None, None)

    def v_gap(self):
        self.p.run("vector", lambda e: e.tensor_copy(out=self.junk[:, 0:1], in_=self.junk[:, 1:2]))

    def s_gap(self):
        self.p.run("scalar", lambda e: e.copy(out=self.junk[:, 2:3], in_=self.junk[:, 3:4]))

    def pe(self, kind):
        pass

    def dump(self, name, ap, shape, dt=F32):
        if name not in self.dbg_names:
            return
        o = self.nc.dram_tensor("dbg_" + name, shape, dt, kind="ExternalOutput").ap()
        self.dbg_outs[name] = o
        self.p.barrier()
        self.p.dma("sync", o, ap, "dOut")
        self.p.barrier()

    def phase0(self):
        p = self.p
        d = self.d
        p.dma("sync", self.X[:, :, :], d["x_fm"].rearrange("(c p) t -> p c t", p=128), "dIn")
        for src, dst in (("identf", self.c_identf), ("onesf", self.c_onesf),
                         ("identb", self.c_identb), ("negidentb", self.c_negidentb),
                         ("onesb", self.c_onesb), ("negmaskU_T", self.c_mU),
                         ("negmaskI_T", self.c_mI), ("negmaskL_T", self.c_mL)):
            p.dma("sync", dst[:, :], d[src][:, :], "dIn")
        p.dma("sync", self.c_E8[:, :, :], d["E8_row"][:, :, :], "dIn")
        p.dma("sync", self.c_cols[:, :], d["ccols"][:, :], "dIn")
        p.dma("sync", self.c_Acol[:, :], d["A_col"][:, :], "dIn")
        p.dma("sync", self.c_dtcol[:, :], d["dt_col"][:, :], "dIn")
        p.dma("sync", self.c_normw[:, :], d["normw_col"][:, :], "dIn")
        p.dma("sync", self.ebias_row_sb[:, :], d["ebias_row"][:, :], "dIn")
        p.dma("sync", self.oh_row_sb[:, :], d["onehot"][:, :], "dIn")
        p.dma("sync", self.ohp_row_sb[:, :], d["onehot_prev"][:, :], "dIn")
        vin = p.cnt["dIn"]
        p.run("vector", lambda e: e.memset(self.junk[:, :], 0.0))
        self.pe("mm")
        p.wait("tensor", "dIn", vin)
        for row, dst in ((self.ebias_row_sb, self.c_ebias_b),
                         (self.oh_row_sb, self.c_oh), (self.ohp_row_sb, self.c_ohp)):
            def mmf(e, row=row):
                return e.matmul(self.bank(5, 0, 8), self.c_onesf[0:1, :], row[0:1, :],
                                start=True, stop=True)
            p.run_sig("tensor", mmf, "tC")
            tv = p.cnt["tC"]
            p.wait("vector", "tC", tv)
            p.run_sig("vector", lambda e, dst=dst: e.tensor_copy(out=dst[:, :], in_=self.bank(5, 0, 8)), "vC")
            p.wait("tensor", "vC", p.cnt["vC"])
        p.wait("scalar", "dIn", vin)
        p.run_sig("scalar", lambda e: e.activation(out=self.c_negexpA[:, :], in_=self.c_Acol[:, :],
                                                   func=AF.Exp), "sC")
        p.wait("vector", "sC", p.cnt["sC"])
        p.run_sig("vector", lambda e: e.tensor_scalar(out=self.c_negexpA[:, :], in0=self.c_negexpA[:, :],
                                                      scalar1=-1.0, scalar2=None, op0=ALU.mult), "vC")

    def layernorm(self, src, dst, tag):
        p = self.p
        self.pe("mm")
        ln_xb, ln_sq = self.ln_xb, self.ln_sq
        ln_mean, ln_var, ln_t = self.ln_mean, self.ln_var, self.ln_t
        if "dIn" in p.cnt:
            p.wait("scalar", "dIn", p.cnt["dIn"])
            p.wait("vector", "dIn", p.cnt["dIn"])
        if not hasattr(self, "_ln_tv"):
            self._ln_tv = []
        for h in range(2):
            hsl = slice(h * TH, (h + 1) * TH)
            for ch in range(DC):
                buf = ch % 2
                gi = len(self._ln_tv)
                if gi >= 2:
                    p.wait("scalar", "tC", self._ln_tv[gi - 2])
                p.run(
                    "scalar", lambda e, ch=ch, buf=buf, hsl=hsl: e.copy(
                        out=ln_xb[:, buf, :], in_=src[:, ch, hsl]))
                p.run_sig(
                    "scalar", lambda e, ch=ch, buf=buf, hsl=hsl: e.activation(
                        out=ln_sq[:, buf, :], in_=src[:, ch, hsl], func=AF.Square), "sC")
                sv = p.cnt["sC"]
                p.wait("tensor", "sC", sv)

                def mm1(e, buf=buf, ch=ch):
                    return e.matmul(self.bank(0), self.c_onesb[:, :], ln_xb[:, buf, :],
                                    start=(ch == 0), stop=(ch == DC - 1))
                p.run("tensor", mm1)

                def mm2(e, buf=buf, ch=ch):
                    return e.matmul(self.bank(1), self.c_onesb[:, :], ln_sq[:, buf, :],
                                    start=(ch == 0), stop=(ch == DC - 1))
                tv = p.run_sig("tensor", mm2, "tC")
                self._ln_tv.append(tv)
            tv = p.cnt["tC"]
            p.wait("vector", "tC", tv)
            p.run("vector", lambda e: e.tensor_scalar(out=ln_var[:, :], in0=self.bank(1),
                                                      scalar1=1.0 / D, scalar2=None, op0=ALU.mult))
            p.run_sig("vector", lambda e: e.tensor_scalar(out=ln_mean[:, :], in0=self.bank(0),
                                                          scalar1=1.0 / D, scalar2=None, op0=ALU.mult), "vC")
            vv = p.cnt["vC"]
            p.wait("scalar", "vC", vv)
            p.run_sig("scalar", lambda e: e.activation(out=ln_t[:, 0, :], in_=ln_mean[:, :],
                                                       func=AF.Square), "sC")
            p.wait("vector", "sC", p.cnt["sC"])
            p.run_sig("vector", lambda e: e.tensor_sub(ln_var[:, :], ln_var[:, :], ln_t[:, 0, :]), "vC")
            p.wait("scalar", "vC", p.cnt["vC"])
            p.run_sig("scalar", lambda e: e.activation(out=ln_var[:, :], in_=ln_var[:, :],
                                                       func=AF.Sqrt, bias=self.c_cols[:, 0:1]), "sC")
            p.wait("vector", "sC", p.cnt["sC"])
            p.run_sig("vector", lambda e: e.reciprocal(out=ln_var[:, :], in_=ln_var[:, :]), "vC")
            for ch0 in range(0, DC, 2):
                for j in (0, 1):
                    p.run("vector", lambda e, ch=ch0 + j, j=j, hsl=hsl: e.tensor_sub(
                        ln_t[:, j, :], src[:, ch, hsl], ln_mean[:, :]))
                for j in (0, 1):
                    p.run("vector", lambda e, ch=ch0 + j, j=j, hsl=hsl: e.tensor_mul(
                        dst[:, ch, hsl], ln_t[:, j, :], ln_var[:, :]))
            p.sig("vector", "vC")
            p.wait("tensor", "vC", p.cnt["vC"])
        self.dump("ln_" + tag, dst[:, :, :], [128, DC, T], BF16)

    def ffn(self, X, call, final=False, only_masks=False):
        p = self.p
        d = self.d
        nc = self.nc
        cms = []

        def al(name, shape, dt):
            cm = nc.sbuf_tensor(f"{name}_{call}", shape, dt)
            h = cm.__enter__()
            cms.append(cm)
            return h
        htil = al("htil", [128, 3, HC, T], BF16)
        w13b = al("w13b", [128, 2, 2, DC, 128], BF16)
        w2b = al("w2b", [128, 2, 3, HC, 128], BF16)
        sbc = al("sbc", [128, T], F32)
        sfm = al("sfm", [8, T], F32)
        shat_fm = al("shat_fm", [8, T], F32)
        shat_tm = al("shat_tm", [128, NCH, 8], F32)
        scrb = al("scrb", [128, 8], F32)
        scraw = al("scraw", [128, 8], F32)
        gta = al("gta", [128, 8, 8], F32)
        eqa = al("eqa", [128, 8, 8], F32)
        rk = al("rk", [128, 8], F32)
        rk2 = al("rk2", [128, 8], F32)
        m1s = al("m1s", [128, 1, TH], F32)
        h1s = al("h1s", [128, 1, TH], BF16)
        if call == 0:
            p.dma("gpsimd", self.gate_sb[:, :, :], d["gate_t"][:, :, :], "dW")

        self.pe("mm")
        p.wait("tensor", "dW", p.cnt["dW"])
        p.wait("tensor", "vC", p.cnt["vC"])
        for h in range(2):
            hsl = slice(h * TH, (h + 1) * TH)

            def mmg(e, h=h, hsl=hsl):
                for k in range(DC):
                    r = e.matmul(self.bank(2 + h)[0:8, :], self.gate_sb[:, k, :],
                                 self.lnb[:, k, hsl], start=(k == 0), stop=(k == DC - 1))
                return r
            p.run_sig("tensor", mmg, "tC")
            p.wait("scalar", "tC", p.cnt["tC"])
            p.run_sig("scalar", lambda e, h=h, hsl=hsl: e.activation(
                out=sfm[0:8, hsl], in_=self.bank(2 + h)[0:8, :], func=AF.Sigmoid), "sC")
        p.wait("tensor", "sC", p.cnt["sC"])

        for ch in range(NCH):
            csl = slice(ch * C, (ch + 1) * C)
            self.pe("tr")
            bk = 2 + ch % 2

            def tr1(e, ch=ch, csl=csl, bk=bk):
                return e.matmul(self.bank(bk)[:, 0:8], sfm[0:8, csl], self.c_identf[0:8, 0:8],
                                start=True, stop=True)
            if ch >= 2:
                p.wait("tensor", "vC", self._ffn_msk_vc[ch - 2])
            p.run_sig("tensor", tr1, "tC")
            p.wait("vector", "tC", p.cnt["tC"])
            p.run("vector", lambda e, bk=bk: e.tensor_add(scrb[:, :], self.bank(bk)[:, 0:8], self.c_ebias_b[:, :]))
            p.run("vector", lambda e, bk=bk: e.tensor_copy(out=scraw[:, :], in_=self.bank(bk)[:, 0:8]))
            if ch == 0 and f"scr0_{call}" in self.dbg_names:
                p.run("vector", lambda e: e.tensor_copy(out=self.scr0[:, 0:8], in_=scrb[:, :]))
                p.run("vector", lambda e: e.tensor_copy(out=self.scr0[:, 8:16], in_=scraw[:, :]))
            for e_ in range(8):
                p.run("vector", lambda e, e_=e_: e.tensor_scalar(
                    out=gta[:, e_, :], in0=scrb[:, :], scalar1=scrb[:, e_:e_ + 1],
                    scalar2=None, op0=ALU.is_gt))
            p.run("vector", lambda e: e.memset(rk2[:, 0:1], 0.0))
            for e_ in range(1, 8):
                p.run("vector", lambda e, e_=e_: e.tensor_scalar(
                    out=eqa[:, e_, 0:e_], in0=scrb[:, 0:e_], scalar1=scrb[:, e_:e_ + 1],
                    scalar2=None, op0=ALU.is_equal))
            for e_ in range(8):
                p.run("vector", lambda e, e_=e_: e.reduce_sum(rk[:, e_:e_ + 1], gta[:, e_, :], axis=AX.X))
            for e_ in range(1, 8):
                p.run("vector", lambda e, e_=e_: e.reduce_sum(rk2[:, e_:e_ + 1], eqa[:, e_, 0:e_], axis=AX.X))
            self.v_gap()
            p.run("vector", lambda e: e.tensor_add(rk[:, :], rk[:, :], rk2[:, :]))
            self.v_gap()
            p.run("vector", lambda e: e.tensor_scalar(
                out=rk[:, :], in0=rk[:, :], scalar1=2.0, scalar2=None, op0=ALU.is_lt))
            self.v_gap()
            p.run("vector", lambda e, ch=ch: e.tensor_mul(
                shat_tm[:, ch, :], rk[:, :], scraw[:, :]))
            vv = p.sig("vector", "vC")
            if not hasattr(self, "_ffn_msk_vc"):
                self._ffn_msk_vc = {}
            self._ffn_msk_vc[ch] = vv
            p.wait("tensor", "vC", vv)

            def tr2(e, ch=ch, bk=bk):
                return e.matmul(self.bank(bk)[0:8, 0:128], shat_tm[:, ch, :], self.c_identf[:, :],
                                start=True, stop=True)
            p.run_sig("tensor", tr2, "tC")
            p.wait("vector", "tC", p.cnt["tC"])
            vv2 = p.run_sig("vector", lambda e, ch=ch, csl=csl, bk=bk: e.tensor_copy(
                out=shat_fm[0:8, csl], in_=self.bank(bk)[0:8, 0:128]), "vC")
            self._ffn_msk_vc[ch] = vv2
        self.dump(f"scr0_{call}", self.scr0[:, :], [128, 16])
        self.dump(f"shat_fm_{call}", shat_fm[0:8, :], [8, T])
        self.dump(f"shat_tm_{call}", shat_tm[:, :, :], [128, NCH, 8])
        self.dump(f"scores_fm_{call}", sfm[0:8, :], [8, T])
        if only_masks:
            self.p.barrier()
            for cm in reversed(cms):
                cm.__exit__(None, None, None)
            return

        slab_tv = {}
        w2_tv = {}
        uitem = 0
        uF_base = self.p.cnt.get("uF", 0)
        self.p.sem("uF")
        for g in range(3):
            egrp = [3 * g, 3 * g + 1, 3 * g + 2]
            hready_v = None
            for ei, e_ in enumerate(egrp):
                self.pe("mm")
                if e_ < 8:
                    p.wait("tensor", "vC", p.cnt["vC"])
                    p.wait("tensor", "oF", p.cnt.get("oF", 0))
                    for h in range(2):
                        hsl = slice(h * TH, (h + 1) * TH)

                        def mmsb(e, e_=e_, h=h, hsl=hsl):
                            return e.matmul(self.bank(4 + h), self.c_E8[:, e_, :],
                                            shat_fm[0:8, hsl], start=True, stop=True)
                        p.run_sig("tensor", mmsb, "tC")
                    p.wait("vector", "tC", p.cnt["tC"])
                    for h in range(2):
                        hsl = slice(h * TH, (h + 1) * TH)
                        p.run("vector", lambda e, h=h, hsl=hsl: e.tensor_copy(
                            out=sbc[:, hsl], in_=self.bank(4 + h)))
                    p.sig("vector", "vC")
                    p.wait("tensor", "vC", p.cnt["vC"])
                for hc in range(HC):
                    si = e_ * HC + hc + call * NE * HC
                    slot = si % 2
                    if si >= 2 and (si - 2) in slab_tv:
                        p.wait("gpsimd", "tC", slab_tv[si - 2])
                    vw1 = p.dma("gpsimd", w13b[:, slot, 0, :, :], d["w1_t"][e_, hc], "dW")
                    vw3 = p.dma("gpsimd", w13b[:, slot, 1, :, :], d["w3_t"][e_, hc], "dW")
                    p.wait("tensor", "dW", vw3)
                    for h in range(2):
                        hsl = slice(h * TH, (h + 1) * TH)
                        pb = uitem % 2
                        if uitem >= 2:
                            p.wait("tensor", "uF", uF_base + uitem - 1)

                        def mmu(e, slot=slot, hsl=hsl, pb=pb):
                            for k in range(DC):
                                e.matmul(self.bank(0 + pb), w13b[:, slot, 0, k, :],
                                         self.lnb[:, k, hsl], start=(k == 0), stop=(k == DC - 1))
                            for k in range(DC):
                                r = e.matmul(self.bank(2 + pb), w13b[:, slot, 1, k, :],
                                             self.lnb[:, k, hsl], start=(k == 0), stop=(k == DC - 1))
                            return r
                        tvu = p.run_sig("tensor", mmu, "tC")
                        if h == 1:
                            slab_tv[si] = tvu
                        if e_ < 8:
                            p.wait("vector", "tC", tvu)
                            p.run_sig("vector", lambda e, pb=pb, hsl=hsl: e.tensor_mul(
                                m1s[:, 0, :], self.bank(0 + pb), sbc[:, hsl]), "vC")
                            p.wait("scalar", "vC", p.cnt["vC"])
                            p.run_sig("scalar", lambda e, pb=pb: e.activation(
                                out=h1s[:, 0, :], in_=m1s[:, 0, :], func=AF.Silu), "sC")
                            p.wait("vector", "sC", p.cnt["sC"])
                            p.run("vector", lambda e, pb=pb: e.tensor_mul(
                                h1s[:, 0, :], h1s[:, 0, :], self.bank(2 + pb)))
                            self.v_gap()
                            p.run("vector", lambda e, pb=pb, ei=ei, hc=hc, hsl=hsl: e.tensor_mul(
                                htil[:, ei, hc, hsl], h1s[:, 0, :], sbc[:, hsl]))
                        else:
                            p.wait("scalar", "tC", tvu)
                            p.run_sig("scalar", lambda e, pb=pb: e.activation(
                                out=h1s[:, 0, :], in_=self.bank(0 + pb), func=AF.Silu), "sC")
                            p.wait("vector", "sC", p.cnt["sC"])
                            p.run("vector", lambda e, pb=pb, ei=ei, hc=hc, hsl=hsl: e.tensor_mul(
                                htil[:, ei, hc, hsl], h1s[:, 0, :], self.bank(2 + pb)))
                        p.sig("vector", "uF")
                        uitem += 1
            hready_total = p.cnt["uF"]
            oitem = 0
            for oc in range(DC):
                wi = (call * 3 + g) * DC + oc
                slot = oc % 2
                if wi >= 2 and (wi - 2) in w2_tv:
                    p.wait("gpsimd", "tC", w2_tv[wi - 2])
                vws = [p.dma("gpsimd", w2b[:, slot, ei, :, :], d["w2_t"][e_g, oc], "dW2")
                       for ei, e_g in enumerate(egrp)]
                p.wait("tensor", "dW2", vws[-1])
                if oc == 0:
                    p.wait("tensor", "uF", hready_total)
                for h in range(2):
                    hsl = slice(h * TH, (h + 1) * TH)
                    pb = 4 + (oitem % 2)
                    if oitem >= 2:
                        p.wait("tensor", "oF", p.cnt["oF"] - 1)

                    def mmw2(e, slot=slot, hsl=hsl, pb=pb):
                        first = True
                        r = None
                        for ei in range(3):
                            for hc in range(HC):
                                r = e.matmul(self.bank(pb), w2b[:, slot, ei, hc, :],
                                             htil[:, ei, hc, hsl], start=first,
                                             stop=(ei == 2 and hc == HC - 1))
                                first = False
                        return r
                    tvw = p.run_sig("tensor", mmw2, "tC")
                    if h == 1:
                        w2_tv[wi] = tvw
                    p.wait("vector", "tC", tvw)
                    vv = p.run_sig("vector", lambda e, oc=oc, hsl=hsl, pb=pb: e.scalar_tensor_tensor(
                        out=X[:, oc, hsl], in0=self.bank(pb), scalar=0.5, in1=X[:, oc, hsl],
                        op0=ALU.mult, op1=ALU.add), "oF")
                    oitem += 1
                    if final and g == 2:
                        p.wait("sync", "oF", vv)
                        p.dma("sync", d["out_fm"][oc * 128:(oc + 1) * 128, hsl], X[:, oc, hsl], "dOut")
        p.barrier()
        for cm in reversed(cms):
            cm.__exit__(None, None, None)

    def attn(self):
        p = self.p
        d = self.d
        nc = self.nc
        p.barrier()
        p.dma("sync", d["x2_sp"].rearrange("c p t -> p c t"), self.X[:, :, :], "dS")
        p.wait("sync", "dS", p.cnt["dS"])
        p.barrier()
        self.free_x()
        cms_keep = []
        cms1 = []
        cms = cms1

        def al(name, shape, dt):
            cm = nc.sbuf_tensor(name, shape, dt)
            h = cm.__enter__()
            cms.append(cm)
            return h
        cms = cms_keep
        wexp = al("wexp", [128, NCH, 4, 32], F32)
        cms = cms1
        Prun = al("Prun", [128, NH, 128], F32)
        Brun = al("Brun", [128, NH, 128], F32)
        GtAll = al("GtAll", [128, NCH, 96], F32)
        g_fm = al("g_fm", [32, T], F32)
        lnB_fm = al("lnB_fm", [32, T], F32)
        ba_fm = al("ba_fm", [64, T], F32)
        bafm2 = ba_fm
        batm = al("batm", [128, 64], F32)
        Gpack = al("Gpack", [96, 128], F32)
        cum = al("cum", [32, 2, 128], F32)
        convpp = al("convpp", [128, MIX_NC, 4], F32)
        qknorm = al("qknorm", [128, 32, TH], BF16)
        pool = al("pool", [128, 6, 515], BF16)
        tails = al("tails", [128, MIX_NC, 3], BF16)
        stash012 = al("stash012", [128, MIX_NC, 3], BF16)
        zstage = al("zstage", [128, 2, TH], BF16)
        vstage = al("vstage", [128, 2, TH], BF16)
        mixed3 = al("mixed3", [128, MIX_NC, 3], F32)
        halo = al("halo", [128, MIX_NC, 3], F32)
        haloscr = al("haloscr", [128, MIX_NC, 3], F32)
        patch = al("patch", [128, MIX_NC, 3], F32)
        ptmp = haloscr
        patchb = al("patchb", [128, MIX_NC, 3], BF16)
        sq3 = al("sq3", [128, 32, 3], BF16)
        postq = al("postq", [128, TH], BF16)
        sqt = al("sqt", [128, TH], BF16)
        rinvt = al("rinvt", [128, TH], F32)
        wqb = al("wqb", [128, 2, DC, 128], BF16)
        wbab = al("wbab", [128, DC, 64], BF16)
        convb = al("convb", [128, 2, 4, 128], BF16)
        DT_U = al("DT_U", [128, HGRP, 128], BF16)
        DT_L = al("DT_L", [128, HGRP, 128], BF16)
        DT_I = al("DT_I", [128, HGRP, 128], BF16)
        B0a = al("B0a", [128, HGRP, 128], F32)
        B0b = al("B0b", [128, HGRP, 128], F32)
        tileb = al("tileb", [128, HGRP, 128], F32)
        Ptk = al("Ptk", [128, HGRP, 128], BF16)
        Ttk = al("Ttk", [128, HGRP, 128], BF16)
        Ptk2 = al("Ptk2", [128, HGRP, 128], BF16)
        Ttk2 = al("Ttk2", [128, HGRP, 128], BF16)
        Rall = al("Rall", [128, HGRP, 256], BF16)
        Ktm = al("Ktm", [128, NKH, 128], BF16)
        vrel = al("vrel", [128, NH, 128], BF16)
        mstage = al("mstage", [128, HGRP, 128], BF16)
        wstage = al("wstage", [128, HGRP, 128], BF16)
        kpt = al("kpt", [128, HGRP, 128], BF16)
        PcT = B0a
        DSc = tileb
        expGt = al("expGt", [128, 32], F32)
        wtmp = al("wtmp", [128, 32], F32)

        p.dma("gpsimd", wbab[:, :, :], d["wba_t"][:, :, :], "dW")
        p.dma("gpsimd", convpp[:, :, :], d["conv_pp"][:, :, :], "dW")
        p.wait("tensor", "dW", p.cnt["dW"])
        p.wait("tensor", "vC", p.cnt["vC"])
        for h in range(2):
            hsl = slice(h * TH, (h + 1) * TH)

            def mmba(e, hsl=hsl):
                for k in range(DC):
                    r = e.matmul(self.bank(5)[0:64, :], wbab[:, k, :], self.lnb[:, k, hsl],
                                 start=(k == 0), stop=(k == DC - 1))
                return r
            p.run_sig("tensor", mmba, "tC")
            p.wait("vector", "tC", p.cnt["tC"])
            p.run_sig("vector", lambda e, hsl=hsl: e.tensor_copy(
                out=ba_fm[:, hsl], in_=self.bank(5)[0:64, :]), "vC")
            p.wait("tensor", "vC", p.cnt["vC"])
        for c in range(NCH):
            csl = slice(c * C, (c + 1) * C)
            p.wait("tensor", "vC", p.cnt["vC"])

            def mmt1(e, csl=csl):
                return e.matmul(self.bank(5)[:, 0:64], ba_fm[:, csl],
                                self.c_identf[0:64, 0:64], start=True, stop=True)
            p.run_sig("tensor", mmt1, "tC")
            p.wait("vector", "tC", p.cnt["tC"])
            bsl = self.bank(5)[:, 0:64].rearrange("p (kh j) -> p kh j", j=4)
            p.run("vector", lambda e, bsl=bsl: e.tensor_copy(
                out=batm[:, 0:32].rearrange("p (kh j) -> p kh j", j=2), in_=bsl[:, :, 0:2]))
            p.run_sig("vector", lambda e, bsl=bsl: e.tensor_copy(
                out=batm[:, 32:64].rearrange("p (kh j) -> p kh j", j=2), in_=bsl[:, :, 2:4]), "vC")
            p.wait("tensor", "vC", p.cnt["vC"])

            def mmt2(e):
                return e.matmul(self.bank(4)[0:64, 0:128], batm[:, :],
                                self.c_identf[:, :], start=True, stop=True)
            p.run_sig("tensor", mmt2, "tC")
            p.wait("vector", "tC", p.cnt["tC"])
            p.run_sig("vector", lambda e, csl=csl: e.tensor_copy(
                out=bafm2[:, csl], in_=self.bank(4)[0:64, 0:128]), "vC")
        p.wait("tensor", "vC", p.cnt["vC"])
        for h in range(2):
            hsl = slice(h * TH, (h + 1) * TH)
            p.wait("scalar", "vC", p.cnt["vC"])
            p.run_sig("scalar", lambda e, hsl=hsl: e.activation(
                out=lnB_fm[:, hsl], in_=bafm2[0:32, hsl], func=AF.Sigmoid), "sC")
        for h in range(2):
            hsl = slice(h * TH, (h + 1) * TH)
            p.run("scalar", lambda e, hsl=hsl: e.activation(
                out=lnB_fm[:, hsl], in_=lnB_fm[:, hsl], func=AF.Ln))
            p.run("scalar", lambda e, hsl=hsl: e.activation(
                out=g_fm[:, hsl], in_=bafm2[32:64, hsl], func=AF.Exp, bias=self.c_dtcol[:, :]))
            p.run_sig("scalar", lambda e, hsl=hsl: e.activation(
                out=g_fm[:, hsl], in_=g_fm[:, hsl], func=AF.Ln,
                bias=self.c_cols[0:32, 1:2]), "sC")
            p.wait("vector", "sC", p.cnt["sC"])
            p.run_sig("vector", lambda e, hsl=hsl: e.tensor_scalar(
                out=g_fm[:, hsl], in0=g_fm[:, hsl], scalar1=self.c_negexpA[:, :],
                scalar2=None, op0=ALU.mult), "vC")
        for c in range(NCH):
            csl = slice(c * C, (c + 1) * C)
            p.run("vector", lambda e, csl=csl: e.tensor_copy(out=cum[:, 0, :], in_=g_fm[:, csl]))
            cur, s_ = 0, 1
            while s_ < C:
                nxt = 1 - cur
                self.v_gap()
                p.run("vector", lambda e, cur=cur, nxt=nxt, s_=s_: e.tensor_copy(
                    out=cum[:, nxt, 0:s_], in_=cum[:, cur, 0:s_]))
                p.run("vector", lambda e, cur=cur, nxt=nxt, s_=s_: e.tensor_add(
                    cum[:, nxt, s_:C], cum[:, cur, s_:C], cum[:, cur, 0:C - s_]))
                cur, s_ = nxt, s_ * 2
            self.v_gap()
            p.run("vector", lambda e, cur=cur: e.tensor_copy(out=Gpack[0:32, :], in_=cum[:, cur, :]))
            p.run("vector", lambda e, cur=cur, csl=csl: e.tensor_add(
                Gpack[32:64, :], cum[:, cur, :], lnB_fm[:, csl]))
            self.v_gap()
            p.run_sig("vector", lambda e, cur=cur: e.tensor_scalar(
                out=Gpack[64:96, :], in0=self.c_onesf[0:32, :], scalar1=cum[:, cur, 127:128],
                scalar2=None, op0=ALU.mult), "vC")
            p.wait("tensor", "vC", p.cnt["vC"])

            def mmgp(e):
                return e.matmul(self.bank(5)[:, 0:96], Gpack[:, :],
                                self.c_identf[0:96, 0:96], start=True, stop=True)
            p.run_sig("tensor", mmgp, "tC")
            p.wait("vector", "tC", p.cnt["tC"])
            p.run_sig("vector", lambda e, c=c: e.tensor_copy(
                out=GtAll[:, c, :], in_=self.bank(5)[:, 0:96]), "vC")
            self.v_gap()
            p.run("vector", lambda e, c=c: e.tensor_sub(
                wtmp[:, 0:32], GtAll[:, c, 64:96], GtAll[:, c, 0:32]))
            p.run_sig("vector", lambda e, c=c: e.tensor_sub(
                wexp[:, c, 1, :], GtAll[:, c, 32:64], GtAll[:, c, 0:32]), "vC")
            p.wait("scalar", "vC", p.cnt["vC"])
            p.run("scalar", lambda e, c=c: e.activation(out=wexp[:, c, 0, :], in_=wtmp[:, 0:32], func=AF.Exp))
            p.run("scalar", lambda e, c=c: e.activation(out=wexp[:, c, 1, :], in_=wexp[:, c, 1, :], func=AF.Exp))
            p.run("scalar", lambda e, c=c: e.activation(out=wexp[:, c, 2, :], in_=GtAll[:, c, 32:64], func=AF.Exp))
            p.run_sig("scalar", lambda e, c=c: e.activation(out=wexp[:, c, 3, :], in_=GtAll[:, c, 64:96], func=AF.Exp), "sC")
        p.wait("vector", "sC", p.cnt["sC"])
        p.run_sig("vector", lambda e: e.memset(tails[:, :, :], 0.0), "vC")

        st = {"qk_tv": {}, "si": 0, "zi": 0, "z_dma": {}, "ci": 0, "conv_tv": {},
              "pool_user": {}, "conv_bank_ev": {}, "ev": {}, "first_chunk": True}

        def emit_a1_half(h):
            hsl = slice(h * TH, (h + 1) * TH)
            if h == 1:
                p.wait("tensor", "sC", p.cnt["sC"])
                p.wait("tensor", "vC", p.cnt["vC"])
                p.wait("vector", "dS", p.cnt["dS"])
                p.wait("scalar", "dS", p.cnt["dS"])
            for oc in range(QKVZ_OC):
                kh, j = oc // 6, oc % 6
                si = st["si"]
                st["si"] += 1
                slot = si % 2
                if (si - 2) in st["qk_tv"]:
                    p.wait("gpsimd", "tC", st["qk_tv"][si - 2])
                vwq = p.dma("gpsimd", wqb[:, slot, :, :], d["wq_t"][oc], "dW")
                p.wait("tensor", "dW", vwq)
                bq = si % 2
                if (si - 2) in st["ev"]:
                    p.wait("tensor", "sC", st["ev"][si - 2])

                def mmq(e, slot=slot, hsl=hsl, bq=bq):
                    for k in range(DC):
                        r = e.matmul(self.bank(bq), wqb[:, slot, k, :], self.lnb[:, k, hsl],
                                     start=(k == 0), stop=(k == DC - 1))
                    return r
                tv = p.run_sig("tensor", mmq, "tC")
                if h == 0 and j < 4:
                    mc3 = (kh if j == 0 else 16 + kh if j == 1 else 32 + 2 * kh + (j - 2))

                    def mmq3(e, slot=slot, mc3=mc3):
                        for k in range(DC):
                            r = e.matmul(self.bank(4)[:, mc3 * 3:mc3 * 3 + 3], wqb[:, slot, k, :],
                                         self.lnb[:, k, T - 3:T], start=(k == 0), stop=(k == DC - 1))
                        return r
                    tv = p.run_sig("tensor", mmq3, "tC")
                st["qk_tv"][si] = tv
                if j >= 4:
                    zc = 2 * kh + (j - 4)
                    zi = st["zi"]
                    st["zi"] += 1
                    zsl = zi % 2
                    if (zi - 2) in st["z_dma"]:
                        p.wait("scalar", "dS", st["z_dma"][zi - 2])
                    p.wait("scalar", "tC", tv)
                    ev = p.run_sig("scalar", lambda e, bq=bq, zsl=zsl: e.copy(
                        out=zstage[:, zsl, :], in_=self.bank(bq)), "sC")
                    st["ev"][si] = ev
                    p.wait("sync", "sC", ev)
                    st["z_dma"][zi] = p.dma(
                        "sync", d["z_sp"][h * 4:(h + 1) * 4, zc].rearrange("c p t -> p c t"),
                        zstage[:, zsl, :].rearrange("p (c t) -> p c t", t=128), "dS")
                    continue
                mc = (kh if j == 0 else 16 + kh if j == 1 else 32 + 2 * kh + (j - 2))
                pslot = mc % 6
                if pslot in st["pool_user"]:
                    tvp, vvp = st["pool_user"][pslot]
                    p.wait("scalar", "tC", tvp)
                    p.wait("scalar", "vC", vvp)
                p.wait("scalar", "tC", tv)
                ev = p.run_sig("scalar", lambda e, bq=bq, pslot=pslot: e.copy(
                    out=pool[:, pslot, 3:515], in_=self.bank(bq)), "sC")
                st["ev"][si] = ev
                ev2 = p.run_sig("scalar", lambda e, pslot=pslot, mc=mc: e.copy(
                    out=pool[:, pslot, 0:3], in_=tails[:, mc, :]), "sC")
                p.wait("vector", "sC", ev2)
                p.run("vector", lambda e, pslot=pslot, mc=mc: e.tensor_copy(
                    out=tails[:, mc, :], in_=pool[:, pslot, 512:515]))
                vvs = p.run_sig("vector", lambda e, pslot=pslot, mc=mc: e.tensor_copy(
                    out=stash012[:, mc, :], in_=pool[:, pslot, 3:6]), "vC")
                ci = st["ci"]
                st["ci"] += 1
                cslot = ci % 2
                if (ci - 2) in st["conv_tv"]:
                    p.wait("gpsimd", "tC", st["conv_tv"][ci - 2])
                vcw = p.dma("gpsimd", convb[:, cslot, :, :], d["conv_diag"][mc], "dW")
                p.wait("tensor", "dW", vcw)
                p.wait("tensor", "vC", vvs)
                bc = 2 + ci % 2
                if bc in st["conv_bank_ev"]:
                    p.wait("tensor", "sC", st["conv_bank_ev"][bc])

                def mmc(e, cslot=cslot, pslot=pslot, bc=bc):
                    for jj in range(4):
                        r = e.matmul(self.bank(bc), convb[:, cslot, jj, :],
                                     pool[:, pslot, jj:jj + 512], start=(jj == 0), stop=(jj == 3))
                    return r
                tvc = p.run_sig("tensor", mmc, "tC")
                st["conv_tv"][ci] = tvc
                st["pool_user"][pslot] = (tvc, vvs)
                p.wait("scalar", "tC", tvc)
                if mc >= 32:
                    vsl = ci % 2
                    ev3 = p.run_sig("scalar", lambda e, bc=bc, vsl=vsl: e.activation(
                        out=vstage[:, vsl, :], in_=self.bank(bc), func=AF.Silu), "sC")
                    st["conv_bank_ev"][bc] = ev3
                    p.wait("sync", "sC", ev3)
                    p.dma("sync", d["v_sp"][mc - 32, :, hsl], vstage[:, vsl, :], "dS")
                else:
                    p.run("scalar", lambda e, bc=bc: e.activation(
                        out=postq[:, :], in_=self.bank(bc), func=AF.Silu))
                    ev3 = p.run_sig("scalar", lambda e: e.activation(
                        out=sqt[:, :], in_=postq[:, :], func=AF.Square), "sC")
                    st["conv_bank_ev"][bc] = ev3
                    p.wait("tensor", "sC", ev3)

                    def mml2(e):
                        return e.matmul(self.bank(5), self.c_onesb[:, :], sqt[:, :],
                                        start=True, stop=True)
                    tvl = p.run_sig("tensor", mml2, "tC")
                    p.wait("scalar", "tC", tvl)
                    scale = float(DK) if mc < 16 else 1.0
                    bcol = self.c_cols[:, 2:3] if mc < 16 else self.c_cols[:, 3:4]
                    p.run_sig("scalar", lambda e, scale=scale, bcol=bcol: e.activation(
                        out=rinvt[:, :], in_=self.bank(5), func=AF.Sqrt, scale=scale, bias=bcol), "sC")
                    p.wait("vector", "sC", p.cnt["sC"])
                    p.run("vector", lambda e: e.reciprocal(out=rinvt[:, :], in_=rinvt[:, :]))
                    self.v_gap()
                    p.run_sig("vector", lambda e, mc=mc: e.tensor_mul(
                        qknorm[:, mc, :], postq[:, :], rinvt[:, :]), "vC")
                    p.wait("tensor", "vC", p.cnt["vC"])
            if h == 0:
                p.wait("vector", "tC", p.cnt["tC"])
                p.run_sig("vector", lambda e: e.tensor_copy(
                    out=mixed3[:, :, :],
                    in_=self.bank(4)[:, 0:192].rearrange("p (m j) -> p m j", j=3)), "vC")
                p.wait("sync", "vC", p.cnt["vC"])
                p.dma("sync", d["ag1_in"].rearrange("m p j -> p m j"), mixed3[:, :, :], "dS")
                p.wait("gpsimd", "dS", p.cnt["dS"])
                p.run_sig("gpsimd", lambda e: e.collective_compute(
                    "AllGather", ALU.bypass, replica_groups=[list(range(NCORES))],
                    ins=[d["ag1_in"].opt()], outs=[d["ag1_out"].opt()]), "cc")
                p.wait("sync", "cc", p.cnt["cc"])
                for dd in range(NCORES):
                    vh = p.dma("sync", haloscr[:, :, :], d["ag1_out"][dd].rearrange("m p j -> p m j"), "dS")
                    p.wait("vector", "dS", vh)
                    if dd == 0:
                        p.run("vector", lambda e: e.tensor_scalar(
                            out=halo[:, :, :], in0=haloscr[:, :, :], scalar1=self.c_ohp[:, 0:1],
                            scalar2=None, op0=ALU.mult))
                    else:
                        p.run("vector", lambda e, dd=dd: e.scalar_tensor_tensor(
                            out=halo[:, :, :], in0=haloscr[:, :, :], scalar=self.c_ohp[:, dd:dd + 1],
                            in1=halo[:, :, :], op0=ALU.mult, op1=ALU.add))
                    vv = p.sig("vector", "vC")
                    p.wait("sync", "vC", vv)
                for t_ in range(3):
                    for jj in range(4):
                        w_ = t_ + jj
                        src_ = halo[:, :, w_:w_ + 1] if w_ < 3 else stash012[:, :, w_ - 3:w_ - 2]
                        if jj == 0:
                            p.run("vector", lambda e, t_=t_, src_=src_: e.tensor_mul(
                                patch[:, :, t_:t_ + 1], src_, convpp[:, :, 0:1]))
                        else:
                            p.run("vector", lambda e, t_=t_, jj=jj, src_=src_: e.tensor_mul(
                                ptmp[:, :, t_:t_ + 1], src_, convpp[:, :, jj:jj + 1]))
                            self.v_gap()
                            p.run("vector", lambda e, t_=t_: e.tensor_add(
                                patch[:, :, t_:t_ + 1], patch[:, :, t_:t_ + 1], ptmp[:, :, t_:t_ + 1]))
                            self.v_gap()
                vvp = p.sig("vector", "vC")
                p.wait("scalar", "vC", vvp)
                p.run("scalar", lambda e: e.activation(out=patchb[:, :, :], in_=patch[:, :, :], func=AF.Silu))
                ev4 = p.run_sig("scalar", lambda e: e.activation(
                    out=sq3[:, :, :], in_=patchb[:, 0:32, :], func=AF.Square), "sC")
                p.wait("tensor", "sC", ev4)

                def mml3(e):
                    for mcq in range(32):
                        r = e.matmul(self.bank(5)[:, mcq * 3:mcq * 3 + 3], self.c_onesb[:, :],
                                     sq3[:, mcq, :], start=True, stop=True)
                    return r
                tvq = p.run_sig("tensor", mml3, "tC")
                p.wait("scalar", "tC", tvq)
                p.run("scalar", lambda e: e.activation(
                    out=rinvt[:, 0:48], in_=self.bank(5)[:, 0:48], func=AF.Sqrt,
                    scale=float(DK), bias=self.c_cols[:, 2:3]))
                ev5 = p.run_sig("scalar", lambda e: e.activation(
                    out=rinvt[:, 48:96], in_=self.bank(5)[:, 48:96], func=AF.Sqrt,
                    scale=1.0, bias=self.c_cols[:, 3:4]), "sC")
                p.wait("vector", "sC", ev5)
                p.run("vector", lambda e: e.reciprocal(out=rinvt[:, 0:96], in_=rinvt[:, 0:96]))
                self.v_gap()
                p.run_sig("vector", lambda e: e.tensor_mul(
                    qknorm[:, 0:32, 0:3],
                    patchb[:, 0:32, :],
                    rinvt[:, 0:96].rearrange("p (m j) -> p m j", j=3)), "vC")
                p.wait("tensor", "vC", p.cnt["vC"])
            p.wait("sync", "vC", p.cnt["vC"])
            p.dma("sync", d["q_sp"][:, :, hsl].rearrange("m p t -> p m t"),
                  qknorm[:, 0:16, :], "dS")
            st["q_spill"] = p.cnt["dS"]

        def emit_a2_chunk(c):
            h = c // 4
            csl = slice((c % 4) * 128, (c % 4) * 128 + 128)
            vv_ = p.dma("scalar", vrel[:, :, :],
                        d["v_sp"][:, :, c * C:(c + 1) * C].rearrange("m p t -> p m t"), "dA")
            if c == 0:
                p.wait("vector", "dA", vv_)
                p.run_sig("vector", lambda e: e.tensor_copy(
                    out=vrel[:, :, 0:3], in_=patchb[:, 32:64, :]), "vC")
            p.wait("tensor", "dA", vv_)
            p.wait("tensor", "vC", p.cnt["vC"])
            p.wait("tensor", "sC", p.cnt["sC"])

            def mmktm(e, csl=csl):
                for kh in range(NKH):
                    r = e.matmul(self.bank(kh // 4)[:, (kh % 4) * 128:(kh % 4) * 128 + 128],
                                 qknorm[:, 16 + kh, csl], self.c_identb[:, :],
                                 start=True, stop=True)
                return r
            tvk = p.run_sig("tensor", mmktm, "tC")
            p.wait("vector", "tC", tvk)
            for b_ in range(4):
                p.run_sig("vector", lambda e, b_=b_: e.tensor_copy(
                    out=Ktm[:, b_ * 4:b_ * 4 + 4, :], in_=self.bank(b_)), "vC")
            p.wait("sync", "vC", p.cnt["vC"])
            p.dma("sync", d["ktm_sp"][c].rearrange("m p t -> p m t"), Ktm[:, :, :], "dS")
            p.wait("tensor", "vC", p.cnt["vC"])
            p.wait("scalar", "vC", p.cnt["vC"])
            p.run_sig("scalar", lambda e, c=c: e.activation(
                out=expGt[:, :], in_=GtAll[:, c, 0:32], func=AF.Exp), "sC")
            p.wait("vector", "sC", p.cnt["sC"])
            for g in range(NGRP):
                h0 = g * HGRP
                for bi, (name, dst) in enumerate((("U", DT_U), ("L", DT_L), ("I", DT_I), ("W", None))):
                    for hh in range(HGRP):
                        hq = h0 + hh
                        if name == "U":
                            fn = lambda e, hh=hh, hq=hq, c=c: e.tensor_scalar(
                                out=tileb[:, hh, :], in0=self.c_mU[:, :],
                                scalar1=GtAll[:, c, 32 + hq:33 + hq], scalar2=None, op0=ALU.add)
                        elif name == "L":
                            fn = lambda e, hh=hh, hq=hq, c=c: e.tensor_scalar(
                                out=tileb[:, hh, :], in0=self.c_mL[:, :],
                                scalar1=GtAll[:, c, hq:hq + 1], scalar2=None, op0=ALU.subtract)
                        elif name == "I":
                            fn = lambda e, hh=hh, hq=hq, c=c: e.tensor_scalar(
                                out=tileb[:, hh, :], in0=self.c_mI[:, :],
                                scalar1=GtAll[:, c, hq:hq + 1], scalar2=None, op0=ALU.add)
                        else:
                            fn = lambda e, hh=hh, hq=hq: e.tensor_scalar(
                                out=tileb[:, hh, :], in0=self.c_onesf[:, :],
                                scalar1=expGt[:, hq:hq + 1], scalar2=None, op0=ALU.mult)
                        p.run("vector", fn)
                    vvt = p.sig("vector", "vC")
                    p.wait("tensor", "vC", vvt)

                    def mmd(e):
                        for hh in range(HGRP):
                            bnk = 4 + hh // 4
                            r = e.matmul(self.bank(bnk)[:, (hh % 4) * 128:(hh % 4) * 128 + 128],
                                         tileb[:, hh, :], self.c_identf[:, :], start=True, stop=True)
                        return r
                    tvd = p.run_sig("tensor", mmd, "tC")
                    p.wait("vector", "tC", tvd)
                    if name == "U":
                        for hh in range(HGRP):
                            p.run("vector", lambda e, hh=hh, hq=h0 + hh, c=c: e.tensor_scalar(
                                out=B0a[:, hh, :], in0=self.c_onesf[:, :],
                                scalar1=GtAll[:, c, hq:hq + 1], scalar2=None, op0=ALU.mult))
                        self.v_gap()
                        for b_ in range(2):
                            p.run("vector", lambda e, b_=b_: e.tensor_sub(
                                tileb[:, b_ * 4:b_ * 4 + 4, :], self.bank(4 + b_), B0a[:, b_ * 4:b_ * 4 + 4, :]))
                    elif name == "L":
                        for hh in range(HGRP):
                            p.run("vector", lambda e, hh=hh, hq=h0 + hh, c=c: e.tensor_scalar(
                                out=B0b[:, hh, :], in0=self.c_onesf[:, :],
                                scalar1=GtAll[:, c, 32 + hq:33 + hq], scalar2=None, op0=ALU.mult))
                        self.v_gap()
                        for b_ in range(2):
                            p.run("vector", lambda e, b_=b_: e.tensor_add(
                                tileb[:, b_ * 4:b_ * 4 + 4, :], self.bank(4 + b_), B0b[:, b_ * 4:b_ * 4 + 4, :]))
                    elif name == "I":
                        for b_ in range(2):
                            p.run("vector", lambda e, b_=b_: e.tensor_sub(
                                tileb[:, b_ * 4:b_ * 4 + 4, :], self.bank(4 + b_), B0a[:, b_ * 4:b_ * 4 + 4, :]))
                    else:
                        for b_ in range(2):
                            p.run("vector", lambda e, b_=b_: e.tensor_copy(
                                out=wstage[:, b_ * 4:b_ * 4 + 4, :], in_=self.bank(4 + b_)))
                    vve = p.sig("vector", "vC")
                    p.wait("tensor", "vC", vve)
                    if name != "W":
                        p.wait("scalar", "vC", vve)
                        p.run_sig("scalar", lambda e, dst=dst: e.activation(
                            out=dst[:, :, :], in_=tileb[:, :, :], func=AF.Exp), "sC")
                        p.wait("vector", "sC", p.cnt["sC"])
                p.wait("sync", "vC", p.cnt["vC"])
                st["wbc_dma"] = p.dma("sync", d["wbc_sp"][c, h0:h0 + HGRP].rearrange("m p t -> p m t"),
                                       wstage[:, :, :], "dS")
                p.wait("tensor", "sC", p.cnt["sC"])

                def mmkk(e, csl=csl, g=g):
                    for i4 in range(4):
                        kh = g * 4 + i4
                        for r_ in range(2):
                            col = (i4 * 2 + r_) * 128
                            e.matmul(self.bank(4 + col // 512)[:, col % 512:col % 512 + 128],
                                     qknorm[:, 16 + kh, csl], qknorm[:, 16 + kh, csl],
                                     start=True, stop=True)
                    for i4 in range(4):
                        kh = g * 4 + i4
                        for r_ in range(2):
                            col = (i4 * 2 + r_) * 128
                            r = e.matmul(self.bank(6 + col // 512)[:, col % 512:col % 512 + 128],
                                         qknorm[:, 16 + kh, csl], qknorm[:, kh, csl],
                                         start=True, stop=True)
                    return r
                tvkk = p.run_sig("tensor", mmkk, "tC")
                p.wait("vector", "tC", tvkk)
                p.wait("vector", "dS", st.get("mt_dma", 0))
                for b_ in range(2):
                    p.run("vector", lambda e, b_=b_: e.tensor_mul(
                        Ttk[:, b_ * 4:b_ * 4 + 4, :], self.bank(4 + b_), DT_U[:, b_ * 4:b_ * 4 + 4, :]))
                for b_ in range(2):
                    p.run("vector", lambda e, b_=b_: e.tensor_mul(
                        Ptk[:, b_ * 4:b_ * 4 + 4, :], self.bank(4 + b_), DT_L[:, b_ * 4:b_ * 4 + 4, :]))
                for b_ in range(2):
                    p.run("vector", lambda e, b_=b_: e.tensor_mul(
                        mstage[:, b_ * 4:b_ * 4 + 4, :], self.bank(6 + b_), DT_I[:, b_ * 4:b_ * 4 + 4, :]))
                vvm = p.sig("vector", "vC")
                p.wait("sync", "vC", vvm)
                p.dma("sync", d["mt_sp"][c, h0:h0 + HGRP].rearrange("m p t -> p m t"),
                      mstage[:, :, :], "dS")
                st["mt_dma"] = p.cnt["dS"]
                p.wait("tensor", "vC", vvm)
                def mmvt(e, g=g):
                    for hh in range(HGRP):
                        bnk = 4 + hh // 4
                        r = e.matmul(self.bank(bnk)[:, (hh % 4) * 128:(hh % 4) * 128 + 128],
                                     vrel[:, g * HGRP + hh, :], self.c_identb[:, :],
                                     start=True, stop=True)
                    return r
                tvv = p.run_sig("tensor", mmvt, "tC")
                p.wait("vector", "tC", tvv)
                for hh in range(HGRP):
                    hq = h0 + hh
                    bnk = 4 + hh // 4
                    p.run("vector", lambda e, hh=hh, hq=hq, bnk=bnk, c=c: e.tensor_scalar(
                        out=Rall[:, hh, 0:128], in0=self.bank(bnk)[:, (hh % 4) * 128:(hh % 4) * 128 + 128],
                        scalar1=wexp[:, c, 1, hq:hq + 1], scalar2=None, op0=ALU.mult))
                self.v_gap()
                p.run("vector", lambda e: e.tensor_scalar(
                    out=Rall[:, :, 0:128], in0=Rall[:, :, 0:128], scalar1=-1.0, scalar2=None, op0=ALU.mult))
                for hh in range(HGRP):
                    hq = h0 + hh
                    kh = hq // 2
                    p.run("vector", lambda e, hh=hh, hq=hq, kh=kh, c=c: e.tensor_scalar(
                        out=Rall[:, hh, 128:256], in0=Ktm[:, kh, :],
                        scalar1=wexp[:, c, 2, hq:hq + 1], scalar2=None, op0=ALU.mult))
                vvr = p.sig("vector", "vC")
                p.wait("tensor", "vC", vvr)
                for lev in range(NLEV):
                    Ps, Ts = (Ptk, Ttk) if lev % 2 == 0 else (Ptk2, Ttk2)
                    Pd, Td = (Ptk2, Ttk2) if lev % 2 == 0 else (Ptk, Ttk)
                    iden = (self.c_negidentb if lev == 0 else self.c_identb)[:, :]

                    def mmapp(e, Ts=Ts, iden=iden):
                        for hh in range(HGRP):
                            bnk = hh // 2
                            col = (hh % 2) * 256
                            e.matmul(self.bank(bnk)[:, col:col + 256], iden, Rall[:, hh, :],
                                     start=True, stop=False)
                            r = e.matmul(self.bank(bnk)[:, col:col + 256], Ts[:, hh, :], Rall[:, hh, :],
                                         start=False, stop=True)
                        return r
                    tva = p.run_sig("tensor", mmapp, "tC")
                    if lev < NLEV - 1:
                        def mmsq(e, Ps=Ps, Ts=Ts):
                            for hh in range(HGRP):
                                bnk = 4 + hh // 2
                                col = (hh % 2) * 256
                                e.matmul(self.bank(bnk)[:, col:col + 128], Ts[:, hh, :], Ps[:, hh, :],
                                         start=True, stop=True)
                                r = e.matmul(self.bank(bnk)[:, col + 128:col + 256], Ps[:, hh, :], Ts[:, hh, :],
                                             start=True, stop=True)
                            return r
                        tvs = p.run_sig("tensor", mmsq, "tC")
                    p.wait("scalar", "tC", tva)
                    for bnk in range(4):
                        p.run("scalar", lambda e, bnk=bnk: e.copy(
                            out=Rall[:, bnk * 2:bnk * 2 + 2, :], in_=self.bank(bnk)))
                    sva = p.sig("scalar", "sC")
                    if lev < NLEV - 1:
                        p.wait("vector", "tC", tvs)
                        for hh in range(HGRP):
                            col = (hh % 2) * 256
                            p.run("vector", lambda e, hh=hh, col=col, Pd=Pd: e.tensor_copy(
                                out=Pd[:, hh, :], in_=self.psA[:, 4 + hh // 2, col:col + 128]))
                            p.run("vector", lambda e, hh=hh, col=col, Td=Td: e.tensor_copy(
                                out=Td[:, hh, :], in_=self.psA[:, 4 + hh // 2, col + 128:col + 256]))
                        svv = p.sig("vector", "vC")
                        p.wait("tensor", "sC", sva)
                        p.wait("tensor", "vC", svv)
                    else:
                        p.wait("tensor", "sC", sva)
                p.wait("sync", "sC", p.cnt["sC"])
                p.dma("sync", d["ua_sp"][c, h0:h0 + HGRP].rearrange("m p t -> p m t"),
                      Rall[:, :, 0:128], "dS")
                def mmwt(e):
                    for hh in range(HGRP):
                        bnk = 4 + hh // 4
                        r = e.matmul(self.bank(bnk)[:, (hh % 4) * 128:(hh % 4) * 128 + 128],
                                     Rall[:, hh, 128:256], self.c_identb[:, :], start=True, stop=True)
                    return r
                tvw = p.run_sig("tensor", mmwt, "tC")
                p.wait("vector", "tC", tvw)
                p.wait("vector", "dS", st.get("wbc_dma", 0))
                for b_ in range(2):
                    p.run("vector", lambda e, b_=b_: e.tensor_copy(
                        out=wstage[:, b_ * 4:b_ * 4 + 4, :], in_=self.bank(4 + b_)))
                vvw = p.sig("vector", "vC")
                p.wait("sync", "vC", vvw)
                p.dma("sync", d["mw_sp"][c, h0:h0 + HGRP].rearrange("m p t -> p m t"),
                      wstage[:, :, :], "dS")
                p.wait("tensor", "vC", vvw)
                for hh in range(HGRP):
                    hq = h0 + hh
                    kh = hq // 2
                    p.run("vector", lambda e, hh=hh, hq=hq, kh=kh, c=c: e.tensor_scalar(
                        out=kpt[:, hh, :], in0=Ktm[:, kh, :],
                        scalar1=wexp[:, c, 0, hq:hq + 1], scalar2=None, op0=ALU.mult))
                vvk = p.sig("vector", "vC")
                p.wait("tensor", "vC", vvk)

                first = st["first_chunk"]

                def mmpc(e, first=first):
                    for hh in range(HGRP):
                        bnk = hh // 2
                        col = (hh % 2) * 256
                        r = e.matmul(self.bank(bnk)[:, col:col + 128], Rall[:, hh, 128:256],
                                     kpt[:, hh, :], start=True, stop=True)
                        if first:
                            r = e.matmul(self.bank(bnk)[:, col + 128:col + 256], kpt[:, hh, :],
                                         Rall[:, hh, 128:256], start=True, stop=True)
                    return r
                tvp = p.run_sig("tensor", mmpc, "tC")

                def mmds(e):
                    for hh in range(HGRP):
                        bnk = 4 + hh // 4
                        r = e.matmul(self.bank(bnk)[:, (hh % 4) * 128:(hh % 4) * 128 + 128],
                                     kpt[:, hh, :], Rall[:, hh, 0:128], start=True, stop=True)
                    return r
                tvds = p.run_sig("tensor", mmds, "tC")
                p.wait("vector", "tC", tvds)
                for hh in range(HGRP):
                    hq = h0 + hh
                    bnk = hh // 2
                    col = (hh % 2) * 256
                    p.run("vector", lambda e, hh=hh, hq=hq, bnk=bnk, col=col, c=c: e.scalar_tensor_tensor(
                        out=PcT[:, hh, :], in0=self.c_identf[:, :], scalar=wexp[:, c, 3, hq:hq + 1],
                        in1=self.bank(bnk)[:, col:col + 128], op0=ALU.mult, op1=ALU.add))
                    if first:
                        p.run("vector", lambda e, hh=hh, hq=hq, h0=h0, bnk=bnk, col=col, c=c: e.scalar_tensor_tensor(
                            out=Prun[:, h0 + hh, :], in0=self.c_identf[:, :], scalar=wexp[:, c, 3, hq:hq + 1],
                            in1=self.bank(bnk)[:, col + 128:col + 256], op0=ALU.mult, op1=ALU.add))
                for hh in range(HGRP):
                    bnk = 4 + hh // 4
                    p.run("vector", lambda e, hh=hh, bnk=bnk: e.tensor_copy(
                        out=DSc[:, hh, :], in_=self.bank(bnk)[:, (hh % 4) * 128:(hh % 4) * 128 + 128]))
                vvc = p.sig("vector", "vC")
                p.wait("tensor", "vC", vvc)
                if first:
                    p.run_sig("vector", lambda e, h0=h0: e.tensor_copy(out=Brun[:, h0:h0 + HGRP, :], in_=DSc[:, :, :]), "vC")
                    p.wait("tensor", "vC", p.cnt["vC"])
                else:
                    def mmrun(e, h0=h0):
                        for hh in range(HGRP):
                            bnk = hh // 2
                            col = (hh % 2) * 256
                            e.matmul(self.bank(bnk)[:, col:col + 128], PcT[:, hh, :],
                                     Prun[:, h0 + hh, :], start=True, stop=True)
                            r = e.matmul(self.bank(bnk)[:, col + 128:col + 256], PcT[:, hh, :],
                                         Brun[:, h0 + hh, :], start=True, stop=True)
                        return r
                    tvr = p.run_sig("tensor", mmrun, "tC")
                    p.wait("vector", "tC", tvr)
                    for hh in range(HGRP):
                        bnk = hh // 2
                        col = (hh % 2) * 256
                        p.run("vector", lambda e, hh=hh, h0=h0, bnk=bnk, col=col: e.tensor_copy(
                            out=Prun[:, h0 + hh, :], in_=self.bank(bnk)[:, col:col + 128]))
                        p.run("vector", lambda e, hh=hh, h0=h0, bnk=bnk, col=col: e.tensor_add(
                            Brun[:, h0 + hh, :], self.bank(bnk)[:, col + 128:col + 256], DSc[:, hh, :]))
                    p.sig("vector", "vC")
                    p.wait("tensor", "vC", p.cnt["vC"])
            if c == 0:
                st["first_chunk"] = False

        emit_a1_half(0)
        for c in range(4):
            emit_a2_chunk(c)
        emit_a1_half(1)
        for c in range(4, NCH):
            emit_a2_chunk(c)

        for g in range(NGRP):
            h0 = g * HGRP
            p.wait("tensor", "vC", p.cnt["vC"])

            def mmpt(e, h0=h0):
                for hh in range(HGRP):
                    bnk = hh // 2
                    col = (hh % 2) * 256
                    r = e.matmul(self.bank(bnk)[:, col:col + 128], Prun[:, h0 + hh, :],
                                 self.c_identf[:, :], start=True, stop=True)
                return r
            tvt = p.run_sig("tensor", mmpt, "tC")
            p.wait("vector", "tC", tvt)
            for hh in range(HGRP):
                bnk = hh // 2
                col = (hh % 2) * 256
                p.run("vector", lambda e, hh=hh, bnk=bnk, col=col: e.tensor_copy(
                    out=PcT[:, hh, :], in_=self.bank(bnk)[:, col:col + 128]))
            vvt2 = p.sig("vector", "vC")
            p.wait("sync", "vC", vvt2)
            p.dma("sync", d["ag2_in"][0, h0:h0 + HGRP].rearrange("m p t -> p m t"), PcT[:, :, :], "dS")
            p.wait("tensor", "vC", vvt2)
        p.wait("sync", "vC", p.cnt["vC"])
        p.dma("sync", d["ag2_in"][1].rearrange("m p t -> p m t"), Brun[:, :, :], "dS")
        p.wait("gpsimd", "dS", p.cnt["dS"])
        p.run_sig("gpsimd", lambda e: e.collective_compute(
            "AllGather", ALU.bypass, replica_groups=[list(range(NCORES))],
            ins=[d["ag2_in"].opt()], outs=[d["ag2_out"].opt()]), "cc")
        p.barrier()
        for cm in reversed(cms1):
            cm.__exit__(None, None, None)
        cms2 = []
        cms = cms2
        Sst = al("Sst", [128, NH, 128], F32)
        Sb16 = al("Sb16", [128, NH, 128], BF16)
        cms2a = []
        cms = cms2a
        runS = al("runS", [128, NH, 128], F32)
        pbufP = al("pbufP", [128, NH, 128], F32)
        pbufB = al("pbufB", [128, NH, 128], F32)
        cms = cms2
        p.run("vector", lambda e: e.memset(Sst[:, :, :], 0.0))
        p.run_sig("vector", lambda e: e.memset(runS[:, :, :], 0.0), "vC")
        p.wait("sync", "cc", p.cnt["cc"])
        for dd in range(NCORES):
            vp = p.dma("sync", pbufP[:, :, :], d["ag2_out"][dd, 0].rearrange("m p t -> p m t"), "dS")
            vb = p.dma("sync", pbufB[:, :, :], d["ag2_out"][dd, 1].rearrange("m p t -> p m t"), "dS")
            if dd > 0:
                p.run_sig("vector", lambda e, dd=dd: e.scalar_tensor_tensor(
                    out=Sst[:, :, :], in0=runS[:, :, :], scalar=self.c_oh[:, dd:dd + 1],
                    in1=Sst[:, :, :], op0=ALU.mult, op1=ALU.add), "vC")
            p.wait("vector", "dS", vb)
            if dd == 0:
                p.run_sig("vector", lambda e: e.tensor_copy(out=runS[:, :, :], in_=pbufB[:, :, :]), "vC")
                p.wait("sync", "vC", p.cnt["vC"])
                p.wait("tensor", "vC", p.cnt["vC"])
            elif dd < NCORES - 1:
                p.wait("tensor", "dS", vb)
                p.wait("tensor", "vC", p.cnt["vC"])

                def mmpre(e):
                    for hq in range(NH):
                        bnk = hq % 8
                        col = ((hq // 8) % 4) * 128
                        r = e.matmul(self.bank(bnk)[:, col:col + 128], pbufP[:, hq, :],
                                     runS[:, hq, :], start=True, stop=True)
                    return r
                tvpre = p.run_sig("tensor", mmpre, "tC")
                p.wait("vector", "tC", tvpre)
                for hq in range(NH):
                    bnk = hq % 8
                    col = ((hq // 8) % 4) * 128
                    p.run("vector", lambda e, hq=hq, bnk=bnk, col=col: e.tensor_add(
                        runS[:, hq, :], self.bank(bnk)[:, col:col + 128], pbufB[:, hq, :]))
                vpr = p.sig("vector", "vC")
                p.wait("sync", "vC", vpr)
                p.wait("tensor", "vC", vpr)
        p.run_sig("vector", lambda e: e.tensor_copy(out=Sb16[:, :, :], in_=Sst[:, :, :]), "vC")
        self.dump("S_in", Sst[:, :, :], [128, NH, 128])
        p.barrier()
        for cm in reversed(cms2a):
            cm.__exit__(None, None, None)

        self.c_Ecol = al("c_Ecol", [128, NH, 128], BF16)
        self.c_Erow = al("c_Erow", [128, NH, 128], BF16)
        p.dma("sync", self.c_Ecol[:, :, :], d["E_col"].rearrange("m p t -> p m t"), "dS")
        p.dma("sync", self.c_Erow[:, :, :], d["E_row"].rearrange("m p t -> p m t"), "dS")
        mtb = al("mtb", [128, NH, 128], BF16)
        uab = al("uab", [128, NH, 128], BF16)
        mwb = al("mwb", [128, NH, 128], BF16)
        wbcb = al("wbcb", [128, NH, 128], BF16)
        qrel = al("qrel", [128, NKH, 128], BF16)
        ktmb = al("ktmb", [128, NKH, 128], BF16)
        zrel = al("zrel", [128, NH, 128], BF16)
        szt = al("szt", [128, NH, 128], BF16)
        qwt = al("qwt", [128, NH, 128], BF16)
        ub = al("ub", [128, NH, 128], BF16)
        xgt = al("xgt", [128, NH, 128], BF16)
        sqg = al("sqg", [128, NH, 128], BF16)
        rmst = al("rmst", [128, 128], F32)
        rinvb = al("rinvb", [128, 128], BF16)
        yt = al("yt", [128, NH, 128], BF16)
        kptB = al("kpt2", [128, HGRP, 128], BF16)
        p.wait("scalar", "dS", p.cnt["dS"])
        p.wait("tensor", "dS", p.cnt["dS"])
        for c in range(NCH):
            gsl = slice(c * C, (c + 1) * C)
            vmt = p.dma("scalar", mtb[:, :, :], d["mt_sp"][c].rearrange("m p t -> p m t"), "dA")
            vua = p.dma("scalar", uab[:, :, :], d["ua_sp"][c].rearrange("m p t -> p m t"), "dA")
            vmw = p.dma("scalar", mwb[:, :, :], d["mw_sp"][c].rearrange("m p t -> p m t"), "dA")
            vwb = p.dma("scalar", wbcb[:, :, :], d["wbc_sp"][c].rearrange("m p t -> p m t"), "dA")
            vq = p.dma("scalar", qrel[:, :, :], d["q_sp"][:, :, gsl].rearrange("m p t -> p m t"), "dA")
            vkt = p.dma("scalar", ktmb[:, :, :], d["ktm_sp"][c].rearrange("m p t -> p m t"), "dA")
            vz = p.dma("scalar", zrel[:, :, :],
                       d["z_sp"][c].rearrange("m p t -> p m t"), "dA")
            p.wait("scalar", "dA", vz)
            svz = p.run_sig("scalar", lambda e: e.activation(out=szt[:, :, :], in_=zrel[:, :, :], func=AF.Silu), "sC")
            p.wait("vector", "dA", vq)
            for hq in range(NH):
                p.run("vector", lambda e, hq=hq: e.tensor_mul(
                    qwt[:, hq, :], qrel[:, hq // 2, :], wbcb[:, hq, :]))
            vqw = p.sig("vector", "vC")
            p.wait("tensor", "dA", vq)
            for g in range(NGRP):
                h0 = g * HGRP
                p.wait("tensor", "vC", p.cnt["vC"])

                def mmu(e, h0=h0):
                    for hh in range(HGRP):
                        bnk = hh // 4
                        col = (hh % 4) * 128
                        e.matmul(self.bank(bnk)[:, col:col + 128], self.c_identb[:, :],
                                 uab[:, h0 + hh, :], start=True, stop=False)
                        r = e.matmul(self.bank(bnk)[:, col:col + 128], mwb[:, h0 + hh, :],
                                     Sb16[:, h0 + hh, :], start=False, stop=True)
                    return r
                tvu = p.run_sig("tensor", mmu, "tC")
                p.wait("scalar", "tC", tvu)
                for bnk in range(2):
                    p.run("scalar", lambda e, h0=h0, bnk=bnk: e.copy(
                        out=ub[:, h0 + bnk * 4:h0 + bnk * 4 + 4, :], in_=self.bank(bnk)))
                svu = p.sig("scalar", "sC")
                p.wait("tensor", "sC", svu)
                def mmo(e, h0=h0):
                    for hh in range(HGRP):
                        bnk = 2 + hh // 4
                        col = (hh % 4) * 128
                        e.matmul(self.bank(bnk)[:, col:col + 128], Sb16[:, h0 + hh, :],
                                 qwt[:, h0 + hh, :], start=True, stop=False)
                        r = e.matmul(self.bank(bnk)[:, col:col + 128], ub[:, h0 + hh, :],
                                     mtb[:, h0 + hh, :], start=False, stop=True)
                    return r
                tvo = p.run_sig("tensor", mmo, "tC")
                p.wait("vector", "tC", tvo)
                p.wait("vector", "sC", svz)
                for bnk in range(2):
                    p.run("vector", lambda e, h0=h0, bnk=bnk: e.tensor_mul(
                        xgt[:, h0 + bnk * 4:h0 + bnk * 4 + 4, :], self.bank(2 + bnk),
                        szt[:, h0 + bnk * 4:h0 + bnk * 4 + 4, :]))
                vvo = p.sig("vector", "vC")
                p.wait("tensor", "vC", vvo)
                for hh in range(HGRP):
                    hq = h0 + hh
                    p.run("vector", lambda e, hh=hh, hq=hq, c=c: e.tensor_scalar(
                        out=kptB[:, hh, :], in0=ktmb[:, hq // 2, :],
                        scalar1=wexp[:, c, 0, hq:hq + 1], scalar2=None, op0=ALU.mult))
                vvk2 = p.sig("vector", "vC")
                p.wait("tensor", "vC", vvk2)

                def mms(e, h0=h0):
                    for hh in range(HGRP):
                        bnk = 4 + hh // 4
                        col = (hh % 4) * 128
                        r = e.matmul(self.bank(bnk)[:, col:col + 128], kptB[:, hh, :],
                                     ub[:, h0 + hh, :], start=True, stop=True)
                    return r
                tvs = p.run_sig("tensor", mms, "tC")
                p.wait("vector", "tC", tvs)
                for hh in range(HGRP):
                    hq = h0 + hh
                    bnk = 4 + hh // 4
                    col = (hh % 4) * 128
                    p.run("vector", lambda e, hh=hh, hq=hq, bnk=bnk, col=col, c=c: e.scalar_tensor_tensor(
                        out=Sst[:, hq, :], in0=Sst[:, hq, :], scalar=wexp[:, c, 3, hq:hq + 1],
                        in1=self.bank(bnk)[:, col:col + 128], op0=ALU.mult, op1=ALU.add))
                vvs2 = p.sig("vector", "vC")
                p.wait("tensor", "vC", vvs2)
            self.v_gap()
            p.run_sig("vector", lambda e: e.tensor_copy(out=Sb16[:, :, :], in_=Sst[:, :, :]), "vC")
            p.wait("scalar", "vC", p.cnt["vC"])
            svq = p.run_sig("scalar", lambda e: e.activation(out=sqg[:, :, :], in_=xgt[:, :, :], func=AF.Square), "sC")
            p.wait("tensor", "sC", svq)

            def mmssq(e):
                for hq in range(NH):
                    r = e.matmul(self.bank(6)[:, 0:128], self.c_Ecol[:, hq, :], sqg[:, hq, :],
                                 start=(hq == 0), stop=(hq == NH - 1))
                return r
            tvq2 = p.run_sig("tensor", mmssq, "tC")
            p.wait("vector", "tC", tvq2)
            p.run_sig("vector", lambda e: e.tensor_scalar(
                out=rmst[:, :], in0=self.bank(6)[:, 0:128], scalar1=1.0 / 128,
                scalar2=self.c_cols[:, 4:5], op0=ALU.mult, op1=ALU.add), "vC")
            p.wait("scalar", "vC", p.cnt["vC"])
            p.run_sig("scalar", lambda e: e.activation(out=rmst[:, :], in_=rmst[:, :], func=AF.Sqrt), "sC")
            p.wait("vector", "sC", p.cnt["sC"])
            p.run("vector", lambda e: e.reciprocal(out=rmst[:, :], in_=rmst[:, :]))
            self.v_gap()
            p.run_sig("vector", lambda e: e.tensor_copy(out=rinvb[:, :], in_=rmst[:, :]), "vC")
            p.wait("tensor", "vC", p.cnt["vC"])
            for g in range(NGRP):
                h0 = g * HGRP

                def mmrb(e, h0=h0):
                    for hh in range(HGRP):
                        bnk = 6 + hh // 4
                        col = (hh % 4) * 128
                        r = e.matmul(self.bank(bnk)[:, col:col + 128], self.c_Erow[:, h0 + hh, :],
                                     rinvb[:, :], start=True, stop=True)
                    return r
                tvb = p.run_sig("tensor", mmrb, "tC")
                p.wait("vector", "tC", tvb)
                if h0 == 0:
                    p.wait("vector", "dS", st.get("y_dma", 0))
                for bnk in range(2):
                    p.run("vector", lambda e, h0=h0, bnk=bnk: e.tensor_mul(
                        yt[:, h0 + bnk * 4:h0 + bnk * 4 + 4, :], self.bank(6 + bnk),
                        xgt[:, h0 + bnk * 4:h0 + bnk * 4 + 4, :]))
                vvy = p.sig("vector", "vC")
                p.wait("tensor", "vC", vvy)
            p.wait("scalar", "vC", p.cnt["vC"])
            p.wait("scalar", "dS", st.get("y_dma", 0))
            svy = p.run_sig("scalar", lambda e: e.activation(
                out=yt[:, :, :], in_=yt[:, :, :], func=AF.Copy, scale=self.c_normw[:, :]), "sC")
            p.wait("sync", "sC", svy)
            st["y_dma"] = p.dma("sync", d["y_sp"][:, :, c * C:(c + 1) * C].rearrange("m p t -> p m t"), yt[:, :, :], "dS")
        p.barrier()
        for cm in reversed(cms2):
            cm.__exit__(None, None, None)
        for cm in reversed(cms_keep):
            cm.__exit__(None, None, None)

    def rebuild_x3(self):
        p = self.p
        d = self.d
        self.alloc_x()
        X = self.X
        cmx = []

        def al2(n_, s_, t_):
            cm = self.nc.sbuf_tensor(n_, s_, t_)
            h = cm.__enter__()
            cmx.append(cm)
            return h
        wob = al2("wob", [128, 2, 8, DC, 128], BF16)
        yrl = al2("yrl", [128, 2, 8, TH], BF16)
        p.wait("sync", "dS", p.cnt["dS"])
        vx = p.dma("sync", X[:, :, :], d["x2_sp"].rearrange("c p t -> p c t"), "dIn2")
        p.wait("vector", "dIn2", vx)
        evA = {}
        slab_done = {}
        ydma_done = {}
        for kg in range(4):
            ws = kg % 2
            if (kg - 2) in slab_done:
                p.wait("gpsimd", "tC", slab_done[kg - 2])
            vw = p.dma("gpsimd", wob[:, ws, :, :, :], d["wout_t"][:, kg * 8:kg * 8 + 8, :, :], "dW")
            for h in range(2):
                hsl = slice(h * TH, (h + 1) * TH)
                yb = h % 2
                if (kg - 1, h) in ydma_done:
                    p.wait("scalar", "tC", ydma_done[(kg - 1, h)])
                vy = p.dma("scalar", yrl[:, yb, :, :],
                           d["y_sp"][kg * 8:kg * 8 + 8, :, hsl].rearrange("k p t -> p k t"), "dA")
                p.wait("tensor", "dW", vw)
                p.wait("tensor", "dA", vy)
                tvop = None
                for ocg in range(2):
                    for oc in range(ocg * 8, ocg * 8 + 8):
                        bnk = oc % 8
                        if kg > 0:
                            p.wait("tensor", "vC", evA[(h, oc)])

                        def mmop(e, oc=oc, bnk=bnk, yb=yb, ws=ws):
                            for kk in range(8):
                                r = e.matmul(self.bank(bnk), wob[:, ws, kk, oc, :], yrl[:, yb, kk, :],
                                             start=(kk == 0), stop=(kk == 7))
                            return r
                        tvop = p.run_sig("tensor", mmop, "tC")
                        p.wait("vector", "tC", tvop)
                        evA[(h, oc)] = p.run_sig("vector", lambda e, oc=oc, bnk=bnk, hsl=hsl: e.tensor_add(
                            X[:, oc, hsl], X[:, oc, hsl], self.bank(bnk)), "vC")
                ydma_done[(kg, h)] = tvop
            slab_done[kg] = tvop
        for cm in reversed(cmx):
            cm.__exit__(None, None, None)
        p.barrier()

    def finale(self):
        p = self.p
        p.wait("sync", "dOut", p.cnt["dOut"]) if "dOut" in p.cnt else None
        for s in ("dIn", "dW", "dW2", "dS", "dA"):
            if s in p.cnt:
                p.wait("sync", s, p.cnt[s])


def build(dbg=(), upto="all", races=True):
    nc = bass.Bass("TRN2", target_bir_lowering=False, debug=False, detect_race_conditions=races)
    with nc.Block() as block:
        k = Kern(nc, dbg=dbg)
        k.alloc_x()
        k.phase0()
        k.layernorm(k.X, k.lnb, "1")
        if upto == "ln":
            k.dump("x2", k.X[:, :, :], [128, DC, T])
            k.finale()
            k.p.emit(block)
            k.close()
            return nc
        k.ffn(k.X, call=0, only_masks=(upto == "masks"))
        k.dump("x2", k.X[:, :, :], [128, DC, T])
        if upto in ("ffn1", "masks"):
            k.finale()
            k.p.emit(block)
            k.close()
            return nc
        k.layernorm(k.X, k.lnb, "2")
        k.attn()
        k.rebuild_x3()
        if upto == "attn":
            k.dump("x3", k.X[:, :, :], [128, DC, T])
            k.finale()
            k.p.emit(block)
            k.close()
            return nc
        k.layernorm(k.X, k.lnb, "3")
        k.ffn(k.X, call=1, final=True)
        k.dump("x4", k.X[:, :, :], [128, DC, T])
        k.finale()
        k.p.emit(block)
        k.close()
    return nc


_CACHE = {}


def make_in_maps(sh, cn, per_core):
    import ml_dtypes
    in_maps = []
    for c in range(NCORES):
        m = dict(sh)
        m["identf"] = cn["identf"]
        m["onesf"] = cn["onesf"]
        m["identb"] = cn["identf"].astype(ml_dtypes.bfloat16)
        m["negidentb"] = (-cn["identf"]).astype(ml_dtypes.bfloat16)
        m["onesb"] = cn["onesf"].astype(ml_dtypes.bfloat16)
        m["negmaskU_T"] = cn["negmaskU_T"]
        m["negmaskI_T"] = cn["negmaskI_T"]
        m["negmaskL_T"] = cn["negmaskL_T"]
        m["E_col"] = cn["E_col"].astype(ml_dtypes.bfloat16)
        m["E_row"] = cn["E_row"].astype(ml_dtypes.bfloat16)
        m["E8_row"] = cn["E8_row"]
        m["ccols"] = cn["ccols"]
        m.update(per_core[c])
        in_maps.append(m)
    return in_maps


def kernel(**inputs):
    sh, cn, per_core = host_prep(inputs)
    if "full" not in _CACHE:
        _CACHE["full"] = build()
    nc = _CACHE["full"]
    in_maps = make_in_maps(sh, cn, per_core)
    res = run_bass_kernel_spmd(nc, in_maps, core_ids=list(range(NCORES)))
    out = np.empty((1, NCORES * T, D), np.float32)
    for c in range(NCORES):
        out[0, c * T:(c + 1) * T, :] = res.results[c]["out_fm"].T
    return out


# version 02 (final)
# speedup vs baseline: 1.0436x; 1.0436x over previous
import math
import numpy as np
import concourse.bass as bass
import concourse.mybir as mybir
from concourse.bass_utils import run_bass_kernel_spmd

F32 = mybir.dt.float32
BF16 = mybir.dt.bfloat16
AF = mybir.ActivationFunctionType
ALU = mybir.AluOpType
AX = mybir.AxisListType

NCORES = 8
T = 1024
TH = 512
C = 128
NCH = 8
D = 2048
DC = 16
HID = 1024
HC = 8
NE = 9
NH = 32
NKH = 16
DK = 128
DV = 128
QKVZ_OC = 96
MIX_NC = 64
EPS_LN = 1e-5
EPS_L2 = 1e-6
EPS_RMS = 1e-6
BIG = 1e30
NLEV = 7
HGRP = 8
NGRP = NH // HGRP


def _f32c(x):
    return np.ascontiguousarray(x, dtype=np.float32)


def host_prep(inputs):
    x = np.asarray(inputs["L0_attn_hidden_states"])[0]
    w_qkvz = np.asarray(inputs["L0_attn_w_qkvz"])
    w_ba = np.asarray(inputs["L0_attn_w_ba"])
    conv_w = np.asarray(inputs["L0_attn_conv_w"])
    dt_bias = np.asarray(inputs["L0_attn_dt_bias"])
    A_log = np.asarray(inputs["L0_attn_A_log"])
    norm_w = np.asarray(inputs["L0_attn_norm_w"])
    w_out = np.asarray(inputs["L0_attn_w_out"])
    gate = np.asarray(inputs["L0_ffn_gate"])
    ebias = np.asarray(inputs["L0_ffn_expert_bias"])
    w1 = np.asarray(inputs["L0_ffn_w1"])
    w2 = np.asarray(inputs["L0_ffn_w2"])
    w3 = np.asarray(inputs["L0_ffn_w3"])
    sw1 = np.asarray(inputs["L0_ffn_shared_w1"])
    sw2 = np.asarray(inputs["L0_ffn_shared_w2"])
    sw3 = np.asarray(inputs["L0_ffn_shared_w3"])

    sh = {}
    wq = w_qkvz.reshape(DC, 128, QKVZ_OC, 128)
    sh["wq_t"] = _f32c(wq.transpose(2, 1, 0, 3))
    sh["wba_t"] = _f32c(w_ba.reshape(DC, 128, 64).transpose(1, 0, 2))
    sh["gate_t"] = _f32c(gate.T.reshape(DC, 128, 8).transpose(1, 0, 2))
    w13 = []
    for warr, swarr in ((w1, sw1), (w3, sw3)):
        wa = np.concatenate([warr, swarr[None]], 0)
        wt = wa.reshape(NE, HC, 128, DC, 128)
        w13.append(_f32c(wt.transpose(0, 1, 4, 3, 2)))
    sh["w1_t"], sh["w3_t"] = w13
    w2a = np.concatenate([w2, sw2[None]], 0)
    w2t = w2a.reshape(NE, DC, 128, HC, 128)
    sh["w2_t"] = _f32c(w2t.transpose(0, 1, 4, 3, 2))
    wo = w_out.reshape(NH, 128, DC, 128)
    sh["wout_t"] = _f32c(wo.transpose(1, 0, 2, 3))
    cw = conv_w[:, 0, :].reshape(MIX_NC, 128, 4)
    idx = np.arange(128)
    diag = np.zeros((MIX_NC, 4, 128, 128), np.float32)
    for mc in range(MIX_NC):
        for j in range(4):
            diag[mc, j, idx, idx] = cw[mc, :, j]
    conv_seq = []
    for oc in range(QKVZ_OC):
        kh, j = oc // 6, oc % 6
        if j < 4:
            conv_seq.append(kh if j == 0 else 16 + kh if j == 1 else 32 + 2 * kh + (j - 2))
    diag_t = diag.transpose(0, 2, 1, 3)
    sh["conv_diag"] = _f32c(diag_t[np.array(conv_seq)])
    sh["conv_pp"] = _f32c(cw.transpose(1, 0, 2))
    sh["A_col"] = _f32c(A_log.reshape(32, 1))
    sh["dt_col"] = _f32c(dt_bias.reshape(32, 1))
    sh["normw_col"] = _f32c(norm_w.reshape(128, 1))
    sh["ebias_row"] = _f32c(ebias.reshape(1, 8))

    cn = {}
    cn["identf"] = _f32c(np.eye(128))
    cn["onesf"] = _f32c(np.ones((128, 128)))
    cn["negmaskU_T"] = _f32c(-BIG * (idx[:, None] <= idx[None, :]))
    cn["negmaskI_T"] = _f32c(-BIG * (idx[:, None] < idx[None, :]))
    cn["negmaskL_T"] = _f32c(-BIG * (idx[None, :] <= idx[:, None]))
    E = np.zeros((NH, 128, 128), np.float32)
    E[np.arange(NH), :, np.arange(NH)] = 1.0
    cn["E_col"] = E
    Er = np.zeros((NH, 128, 128), np.float32)
    Er[np.arange(NH), np.arange(NH), :] = 1.0
    cn["E_row"] = Er
    E8 = np.zeros((8, 8, 128), np.float32)
    E8[np.arange(8), np.arange(8), :] = 1.0
    cn["E8_row"] = E8
    ccols = np.zeros((128, 8), np.float32)
    ccols[:, 0] = EPS_LN; ccols[:, 1] = 1.0; ccols[:, 2] = 128 * EPS_L2
    ccols[:, 3] = EPS_L2; ccols[:, 4] = EPS_RMS
    cn["ccols"] = ccols

    per_core = []
    for c in range(NCORES):
        d = {}
        d["x_fm"] = _f32c(x[c * T:(c + 1) * T].T)
        oh = np.zeros((1, 8), np.float32); oh[0, c] = 1.0
        d["onehot"] = oh
        ohp = np.zeros((1, 8), np.float32)
        if c > 0:
            ohp[0, c - 1] = 1.0
        d["onehot_prev"] = ohp
        per_core.append(d)
    return sh, cn, per_core


class Prog:
    ENGINES = ("sync", "scalar", "vector", "tensor", "gpsimd")

    def __init__(self, nc):
        self.nc = nc
        self.q = {e: [] for e in self.ENGINES}
        self._sems = {}
        self.cnt = {}
        self._ctx = []
        self.last = {}
        self.pending = {}
        self.lastsig = {}

    def sem(self, name):
        if name not in self._sems:
            cm = self.nc.semaphore(name)
            self._sems[name] = cm.__enter__()
            self._ctx.append(cm)
            self.cnt[name] = 0
        return self._sems[name]

    def bump(self, name, n=1):
        s = self.sem(name)
        self.cnt[name] += n
        return s, self.cnt[name]

    def on(self, eng, fn):
        self.q[eng].append(fn)

    def dma(self, qeng, out, in_, sem_name):
        s, v = self.bump(sem_name, 16)

        def f(e):
            e.dma_start(out=out, in_=in_).then_inc(s, 16)
        self.on(qeng, f)
        self.pending[qeng] = "dma"
        return v

    def wait(self, eng, sem_name, val):
        s = self.sem(sem_name)
        self.on(eng, lambda e: e.wait_ge(s, val))

    def wait_now(self, eng, sem_name):
        self.wait(eng, sem_name, self.cnt[sem_name])

    PRIM = {"tensor": "tC", "vector": "vC", "scalar": "sC", "gpsimd": "gC"}

    def run(self, eng, fn):
        def f(e):
            inst = fn(e)
            if inst is not None:
                self.last[eng] = inst
        self.on(eng, f)
        self.pending[eng] = "inst"

    def run_sig(self, eng, fn, sem_name, n=1):
        s, v = self.bump(sem_name, n)

        def f(e):
            inst = fn(e)
            inst.then_inc(s, n)
            self.last[eng] = inst
        self.on(eng, f)
        self.pending[eng] = None
        self.lastsig[eng] = (sem_name, v)
        return v

    def sig(self, eng, sem_name, n=1):
        assert self.pending.get(eng) == "inst", f"sig on {eng} without pending instruction"
        s, v = self.bump(sem_name, n)

        def f(e):
            self.last[eng].then_inc(s, n)
        self.on(eng, f)
        self.pending[eng] = None
        self.lastsig[eng] = (sem_name, v)
        return v

    def token(self, eng):
        if self.pending.get(eng) == "inst":
            v = self.sig(eng, self.PRIM[eng])
            return (self.PRIM[eng], v)
        return self.lastsig.get(eng)

    def barrier(self):
        dma_sems = [(nm, self.cnt[nm]) for nm in list(self.cnt)
                    if nm.startswith("d") and self.cnt[nm] > 0]
        toks = []
        for eng in ("tensor", "vector", "scalar"):
            t = self.token(eng)
            if t is not None:
                toks.append(t)
        s = self.sem("bar")
        self.cnt["bar"] += 2
        tgt = self.cnt["bar"]
        for eng in ("sync", "gpsimd"):
            self.on(eng, lambda e, s=s: e.nop().then_inc(s, 1))
        for eng in self.ENGINES:
            for nm, val in dma_sems + toks:
                self.wait(eng, nm, val)
            self.on(eng, lambda e, s=s, tgt=tgt: e.wait_ge(s, tgt))

    def emit(self, block):
        decs = {"sync": block.sync, "scalar": block.scalar,
                "vector": block.vector, "tensor": block.tensor,
                "gpsimd": block.gpsimd}
        for eng in self.ENGINES:
            fns = self.q[eng]
            if not fns:
                continue

            def body(e, fns=fns):
                for f in fns:
                    f(e)
            decs[eng](body)


class Kern:

    def __init__(self, nc, dbg=()):
        self.nc = nc
        self.p = Prog(nc)
        self.dbg_names = set(dbg)
        self.dbg_outs = {}
        self._cm = []
        self.pe_mode = "mm"
        self.declare()

    def al(self, kind, name, shape, dt):
        f = self.nc.sbuf_tensor if kind == "s" else self.nc.psum_tensor
        cm = f(name, shape, dt)
        h = cm.__enter__()
        self._cm.append(cm)
        return h

    def close(self):
        if self._x_cms is not None:
            self.free_x()
            self._x_cms = None
        for cm in reversed(self._cm):
            cm.__exit__(None, None, None)
        for cm in reversed(self.p._ctx):
            cm.__exit__(None, None, None)

    def declare(self):
        nc = self.nc
        dram = lambda n, s, d, k=None: (nc.dram_tensor(n, s, d, kind=k) if k
                                        else nc.dram_tensor(n, s, d)).ap()
        DI = lambda n, s, d=F32: dram(n, s, d, "ExternalInput")
        DO = lambda n, s, d=F32: dram(n, s, d, "ExternalOutput")
        DS = lambda n, s, d=F32: dram(n, s, d)
        self.d = d = {}
        d["x_fm"] = DI("x_fm", [D, T])
        d["wq_t"] = DI("wq_t", [QKVZ_OC, 128, DC, 128])
        d["wba_t"] = DI("wba_t", [128, DC, 64])
        d["gate_t"] = DI("gate_t", [128, DC, 8])
        d["w1_t"] = DI("w1_t", [NE, HC, 128, DC, 128])
        d["w3_t"] = DI("w3_t", [NE, HC, 128, DC, 128])
        d["w2_t"] = DI("w2_t", [NE, DC, 128, HC, 128])
        d["wout_t"] = DI("wout_t", [128, NH, DC, 128])
        d["conv_diag"] = DI("conv_diag", [MIX_NC, 128, 4, 128])
        d["conv_pp"] = DI("conv_pp", [128, MIX_NC, 4])
        d["A_col"] = DI("A_col", [32, 1])
        d["dt_col"] = DI("dt_col", [32, 1])
        d["normw_col"] = DI("normw_col", [128, 1])
        d["ebias_row"] = DI("ebias_row", [1, 8])
        d["onehot"] = DI("onehot", [1, 8])
        d["onehot_prev"] = DI("onehot_prev", [1, 8])
        d["identf"] = DI("identf", [128, 128])
        d["onesf"] = DI("onesf", [128, 128])
        d["identb"] = DI("identb", [128, 128], BF16)
        d["negidentb"] = DI("negidentb", [128, 128], BF16)
        d["onesb"] = DI("onesb", [128, 128], BF16)
        d["negmaskU_T"] = DI("negmaskU_T", [128, 128])
        d["negmaskI_T"] = DI("negmaskI_T", [128, 128])
        d["negmaskL_T"] = DI("negmaskL_T", [128, 128])
        d["E_col"] = DI("E_col", [NH, 128, 128], BF16)
        d["E_row"] = DI("E_row", [NH, 128, 128], BF16)
        d["E8_row"] = DI("E8_row", [8, 8, 128])
        d["ccols"] = DI("ccols", [128, 8])
        d["out_fm"] = DO("out_fm", [D, T])
        d["x2_sp"] = DS("x2_sp", [DC, 128, T])
        d["z_sp"] = DS("z_sp", [NCH, NH, 128, C], BF16)
        d["q_sp"] = DS("q_sp", [NKH, 128, T], BF16)
        d["ktm_sp"] = DS("ktm_sp", [NCH, NKH, 128, 128], BF16)
        d["mt_sp"] = DS("mt_sp", [NCH, NH, 128, 128], BF16)
        d["ua_sp"] = DS("ua_sp", [NCH, NH, 128, 128], BF16)
        d["mw_sp"] = DS("mw_sp", [NCH, NH, 128, 128], BF16)
        d["wbc_sp"] = DS("wbc_sp", [NCH, NH, 128, 128], BF16)
        d["lnb_sp"] = DS("lnb_sp", [DC, 128, T], BF16)
        d["v_sp"] = DS("v_sp", [NH, 128, T], BF16)
        d["y_sp"] = DS("y_sp", [NH, 128, T], BF16)
        d["ag1_in"] = DS("ag1_in", [MIX_NC, 128, 3])
        d["ag1_out"] = DS("ag1_out", [NCORES, MIX_NC, 128, 3])
        d["ag2_in"] = DS("ag2_in", [2, NH, 128, 128])
        d["ag2_out"] = DS("ag2_out", [NCORES, 2, NH, 128, 128])

        self.c_identf = self.al("s", "c_identf", [128, 128], F32)
        self.c_onesf = self.al("s", "c_onesf", [128, 128], F32)
        self.c_identb = self.al("s", "c_identb", [128, 128], BF16)
        self.c_negidentb = self.al("s", "c_negidentb", [128, 128], BF16)
        self.c_onesb = self.al("s", "c_onesb", [128, 128], BF16)
        self.c_mU = self.al("s", "c_mU", [128, 128], F32)
        self.c_mI = self.al("s", "c_mI", [128, 128], F32)
        self.c_mL = self.al("s", "c_mL", [128, 128], F32)
        self.c_E8 = self.al("s", "c_E8", [8, 8, 128], F32)
        self.c_cols = self.al("s", "c_cols", [128, 8], F32)
        self.scr0 = self.al("s", "scr0", [128, 16], F32)
        self.junk = self.al("s", "junk", [128, 4], F32)
        self.c_Acol = self.al("s", "c_Acol", [32, 1], F32)
        self.c_negexpA = self.al("s", "c_negexpA", [32, 1], F32)
        self.c_dtcol = self.al("s", "c_dtcol", [32, 1], F32)
        self.c_normw = self.al("s", "c_normw", [128, 1], F32)
        self.c_ebias_b = self.al("s", "c_ebias_b", [128, 8], F32)
        self.c_oh = self.al("s", "c_oh", [128, 8], F32)
        self.c_ohp = self.al("s", "c_ohp", [128, 8], F32)
        self.gate_sb = self.al("s", "gate_sb", [128, DC, 8], BF16)
        self.ebias_row_sb = self.al("s", "ebias_row_sb", [1, 8], F32)
        self.oh_row_sb = self.al("s", "oh_row_sb", [1, 8], F32)
        self.ohp_row_sb = self.al("s", "ohp_row_sb", [1, 8], F32)
        self.lnb = self.al("s", "lnb", [128, DC, T], BF16)
        self.X = None
        self._x_cms = None
        self.psA = self.al("p", "psA", [128, 8, 512], F32)

    def bank(self, i, lo=0, n=512):
        return self.psA[:, i, lo:lo + n]

    def alloc_x(self):
        cms = []

        def al2(name, shape, dt):
            cm = self.nc.sbuf_tensor(name, shape, dt)
            h = cm.__enter__()
            cms.append(cm)
            return h
        n = 0 if self._x_cms is None else 1
        self.ln_xb = al2(f"ln_xb{n}", [128, 2, TH], BF16)
        self.ln_sq = al2(f"ln_sq{n}", [128, 2, TH], BF16)
        self.ln_mean = al2(f"ln_mean{n}", [128, TH], F32)
        self.ln_var = al2(f"ln_var{n}", [128, TH], F32)
        self.ln_t = al2(f"ln_t{n}", [128, 2, TH], F32)
        self.X = al2(f"X{n}", [128, DC, T], F32)
        self._x_cms = cms

    def free_x(self):
        for cm in reversed(self._x_cms):
            cm.__exit__(None, None, None)

    def v_gap(self):
        self.p.run("vector", lambda e: e.tensor_copy(out=self.junk[:, 0:1], in_=self.junk[:, 1:2]))

    def s_gap(self):
        self.p.run("scalar", lambda e: e.copy(out=self.junk[:, 2:3], in_=self.junk[:, 3:4]))

    def pe(self, kind):
        pass

    def dump(self, name, ap, shape, dt=F32):
        if name not in self.dbg_names:
            return
        o = self.nc.dram_tensor("dbg_" + name, shape, dt, kind="ExternalOutput").ap()
        self.dbg_outs[name] = o
        self.p.barrier()
        self.p.dma("sync", o, ap, "dOut")
        self.p.barrier()

    def phase0(self):
        p = self.p
        d = self.d
        p.dma("sync", self.X[:, :, :], d["x_fm"].rearrange("(c p) t -> p c t", p=128), "dIn")
        for src, dst in (("identf", self.c_identf), ("onesf", self.c_onesf),
                         ("identb", self.c_identb), ("negidentb", self.c_negidentb),
                         ("onesb", self.c_onesb), ("negmaskU_T", self.c_mU),
                         ("negmaskI_T", self.c_mI), ("negmaskL_T", self.c_mL)):
            p.dma("sync", dst[:, :], d[src][:, :], "dIn")
        p.dma("sync", self.c_E8[:, :, :], d["E8_row"][:, :, :], "dIn")
        p.dma("sync", self.c_cols[:, :], d["ccols"][:, :], "dIn")
        p.dma("sync", self.c_Acol[:, :], d["A_col"][:, :], "dIn")
        p.dma("sync", self.c_dtcol[:, :], d["dt_col"][:, :], "dIn")
        p.dma("sync", self.c_normw[:, :], d["normw_col"][:, :], "dIn")
        p.dma("sync", self.ebias_row_sb[:, :], d["ebias_row"][:, :], "dIn")
        p.dma("sync", self.oh_row_sb[:, :], d["onehot"][:, :], "dIn")
        p.dma("sync", self.ohp_row_sb[:, :], d["onehot_prev"][:, :], "dIn")
        vin = p.cnt["dIn"]
        p.run("vector", lambda e: e.memset(self.junk[:, :], 0.0))
        self.pe("mm")
        p.wait("tensor", "dIn", vin)
        for row, dst in ((self.ebias_row_sb, self.c_ebias_b),
                         (self.oh_row_sb, self.c_oh), (self.ohp_row_sb, self.c_ohp)):
            def mmf(e, row=row):
                return e.matmul(self.bank(5, 0, 8), self.c_onesf[0:1, :], row[0:1, :],
                                start=True, stop=True)
            p.run_sig("tensor", mmf, "tC")
            tv = p.cnt["tC"]
            p.wait("vector", "tC", tv)
            p.run_sig("vector", lambda e, dst=dst: e.tensor_copy(out=dst[:, :], in_=self.bank(5, 0, 8)), "vC")
            p.wait("tensor", "vC", p.cnt["vC"])
        p.wait("scalar", "dIn", vin)
        p.run_sig("scalar", lambda e: e.activation(out=self.c_negexpA[:, :], in_=self.c_Acol[:, :],
                                                   func=AF.Exp), "sC")
        p.wait("vector", "sC", p.cnt["sC"])
        p.run_sig("vector", lambda e: e.tensor_scalar(out=self.c_negexpA[:, :], in0=self.c_negexpA[:, :],
                                                      scalar1=-1.0, scalar2=None, op0=ALU.mult), "vC")

    def layernorm(self, src, dst, tag):
        p = self.p
        self.pe("mm")
        ln_xb, ln_sq = self.ln_xb, self.ln_sq
        ln_mean, ln_var, ln_t = self.ln_mean, self.ln_var, self.ln_t
        if "dIn" in p.cnt:
            p.wait("scalar", "dIn", p.cnt["dIn"])
            p.wait("vector", "dIn", p.cnt["dIn"])
        if not hasattr(self, "_ln_tv"):
            self._ln_tv = []
        for h in range(2):
            hsl = slice(h * TH, (h + 1) * TH)
            for ch in range(DC):
                buf = ch % 2
                gi = len(self._ln_tv)
                if gi >= 2:
                    p.wait("scalar", "tC", self._ln_tv[gi - 2])
                p.run(
                    "scalar", lambda e, ch=ch, buf=buf, hsl=hsl: e.copy(
                        out=ln_xb[:, buf, :], in_=src[:, ch, hsl]))
                p.run_sig(
                    "scalar", lambda e, ch=ch, buf=buf, hsl=hsl: e.activation(
                        out=ln_sq[:, buf, :], in_=src[:, ch, hsl], func=AF.Square), "sC")
                sv = p.cnt["sC"]
                p.wait("tensor", "sC", sv)

                def mm1(e, buf=buf, ch=ch):
                    return e.matmul(self.bank(0), self.c_onesb[:, :], ln_xb[:, buf, :],
                                    start=(ch == 0), stop=(ch == DC - 1))
                p.run("tensor", mm1)

                def mm2(e, buf=buf, ch=ch):
                    return e.matmul(self.bank(1), self.c_onesb[:, :], ln_sq[:, buf, :],
                                    start=(ch == 0), stop=(ch == DC - 1))
                tv = p.run_sig("tensor", mm2, "tC")
                self._ln_tv.append(tv)
            tv = p.cnt["tC"]
            p.wait("vector", "tC", tv)
            p.run("vector", lambda e: e.tensor_scalar(out=ln_var[:, :], in0=self.bank(1),
                                                      scalar1=1.0 / D, scalar2=None, op0=ALU.mult))
            p.run_sig("vector", lambda e: e.tensor_scalar(out=ln_mean[:, :], in0=self.bank(0),
                                                          scalar1=1.0 / D, scalar2=None, op0=ALU.mult), "vC")
            vv = p.cnt["vC"]
            p.wait("scalar", "vC", vv)
            p.run_sig("scalar", lambda e: e.activation(out=ln_t[:, 0, :], in_=ln_mean[:, :],
                                                       func=AF.Square), "sC")
            p.wait("vector", "sC", p.cnt["sC"])
            p.run_sig("vector", lambda e: e.tensor_sub(ln_var[:, :], ln_var[:, :], ln_t[:, 0, :]), "vC")
            p.wait("scalar", "vC", p.cnt["vC"])
            p.run_sig("scalar", lambda e: e.activation(out=ln_var[:, :], in_=ln_var[:, :],
                                                       func=AF.Sqrt, bias=self.c_cols[:, 0:1]), "sC")
            p.wait("vector", "sC", p.cnt["sC"])
            p.run_sig("vector", lambda e: e.reciprocal(out=ln_var[:, :], in_=ln_var[:, :]), "vC")
            for ch0 in range(0, DC, 2):
                for j in (0, 1):
                    p.run("vector", lambda e, ch=ch0 + j, j=j, hsl=hsl: e.tensor_sub(
                        ln_t[:, j, :], src[:, ch, hsl], ln_mean[:, :]))
                for j in (0, 1):
                    p.run("vector", lambda e, ch=ch0 + j, j=j, hsl=hsl: e.tensor_mul(
                        dst[:, ch, hsl], ln_t[:, j, :], ln_var[:, :]))
            p.sig("vector", "vC")
            p.wait("tensor", "vC", p.cnt["vC"])
        self.dump("ln_" + tag, dst[:, :, :], [128, DC, T], BF16)

    def ffn(self, X, call, final=False, only_masks=False):
        p = self.p
        d = self.d
        nc = self.nc
        cms = []

        def al(name, shape, dt):
            cm = nc.sbuf_tensor(f"{name}_{call}", shape, dt)
            h = cm.__enter__()
            cms.append(cm)
            return h
        htil = al("htil", [128, 3, HC, T], BF16)
        w13b = al("w13b", [128, 2, 2, DC, 128], BF16)
        w2b = al("w2b", [128, 2, 3, HC, 128], BF16)
        sbc = al("sbc", [128, T], F32)
        sfm = al("sfm", [8, T], F32)
        shat_fm = al("shat_fm", [8, T], F32)
        shat_tm = al("shat_tm", [128, NCH, 8], F32)
        scrb = al("scrb", [128, 8], F32)
        scraw = al("scraw", [128, 8], F32)
        gta = al("gta", [128, 8, 8], F32)
        eqa = al("eqa", [128, 8, 8], F32)
        rk = al("rk", [128, 8], F32)
        rk2 = al("rk2", [128, 8], F32)
        m1s = al("m1s", [128, 1, TH], F32)
        h1s = al("h1s", [128, 1, TH], BF16)
        if call == 0:
            p.dma("gpsimd", self.gate_sb[:, :, :], d["gate_t"][:, :, :], "dW")

        self.pe("mm")
        p.wait("tensor", "dW", p.cnt["dW"])
        p.wait("tensor", "vC", p.cnt["vC"])
        for h in range(2):
            hsl = slice(h * TH, (h + 1) * TH)

            def mmg(e, h=h, hsl=hsl):
                for k in range(DC):
                    r = e.matmul(self.bank(2 + h)[0:8, :], self.gate_sb[:, k, :],
                                 self.lnb[:, k, hsl], start=(k == 0), stop=(k == DC - 1))
                return r
            p.run_sig("tensor", mmg, "tC")
            p.wait("scalar", "tC", p.cnt["tC"])
            p.run_sig("scalar", lambda e, h=h, hsl=hsl: e.activation(
                out=sfm[0:8, hsl], in_=self.bank(2 + h)[0:8, :], func=AF.Sigmoid), "sC")
        p.wait("tensor", "sC", p.cnt["sC"])

        for ch in range(NCH):
            csl = slice(ch * C, (ch + 1) * C)
            self.pe("tr")
            bk = 2 + ch % 2

            def tr1(e, ch=ch, csl=csl, bk=bk):
                return e.matmul(self.bank(bk)[:, 0:8], sfm[0:8, csl], self.c_identf[0:8, 0:8],
                                start=True, stop=True)
            if ch >= 2:
                p.wait("tensor", "vC", self._ffn_msk_vc[ch - 2])
            p.run_sig("tensor", tr1, "tC")
            p.wait("vector", "tC", p.cnt["tC"])
            p.run("vector", lambda e, bk=bk: e.tensor_add(scrb[:, :], self.bank(bk)[:, 0:8], self.c_ebias_b[:, :]))
            p.run("vector", lambda e, bk=bk: e.tensor_copy(out=scraw[:, :], in_=self.bank(bk)[:, 0:8]))
            if ch == 0 and f"scr0_{call}" in self.dbg_names:
                p.run("vector", lambda e: e.tensor_copy(out=self.scr0[:, 0:8], in_=scrb[:, :]))
                p.run("vector", lambda e: e.tensor_copy(out=self.scr0[:, 8:16], in_=scraw[:, :]))
            for e_ in range(8):
                p.run("vector", lambda e, e_=e_: e.tensor_scalar(
                    out=gta[:, e_, :], in0=scrb[:, :], scalar1=scrb[:, e_:e_ + 1],
                    scalar2=None, op0=ALU.is_gt))
            p.run("vector", lambda e: e.memset(rk2[:, 0:1], 0.0))
            for e_ in range(1, 8):
                p.run("vector", lambda e, e_=e_: e.tensor_scalar(
                    out=eqa[:, e_, 0:e_], in0=scrb[:, 0:e_], scalar1=scrb[:, e_:e_ + 1],
                    scalar2=None, op0=ALU.is_equal))
            for e_ in range(8):
                p.run("vector", lambda e, e_=e_: e.reduce_sum(rk[:, e_:e_ + 1], gta[:, e_, :], axis=AX.X))
            for e_ in range(1, 8):
                p.run("vector", lambda e, e_=e_: e.reduce_sum(rk2[:, e_:e_ + 1], eqa[:, e_, 0:e_], axis=AX.X))
            self.v_gap()
            p.run("vector", lambda e: e.tensor_add(rk[:, :], rk[:, :], rk2[:, :]))
            self.v_gap()
            p.run("vector", lambda e: e.tensor_scalar(
                out=rk[:, :], in0=rk[:, :], scalar1=2.0, scalar2=None, op0=ALU.is_lt))
            self.v_gap()
            p.run("vector", lambda e, ch=ch: e.tensor_mul(
                shat_tm[:, ch, :], rk[:, :], scraw[:, :]))
            vv = p.sig("vector", "vC")
            if not hasattr(self, "_ffn_msk_vc"):
                self._ffn_msk_vc = {}
            self._ffn_msk_vc[ch] = vv
            p.wait("tensor", "vC", vv)

            def tr2(e, ch=ch, bk=bk):
                return e.matmul(self.bank(bk)[0:8, 0:128], shat_tm[:, ch, :], self.c_identf[:, :],
                                start=True, stop=True)
            p.run_sig("tensor", tr2, "tC")
            p.wait("vector", "tC", p.cnt["tC"])
            vv2 = p.run_sig("vector", lambda e, ch=ch, csl=csl, bk=bk: e.tensor_copy(
                out=shat_fm[0:8, csl], in_=self.bank(bk)[0:8, 0:128]), "vC")
            self._ffn_msk_vc[ch] = vv2
        self.dump(f"scr0_{call}", self.scr0[:, :], [128, 16])
        self.dump(f"shat_fm_{call}", shat_fm[0:8, :], [8, T])
        self.dump(f"shat_tm_{call}", shat_tm[:, :, :], [128, NCH, 8])
        self.dump(f"scores_fm_{call}", sfm[0:8, :], [8, T])
        if only_masks:
            self.p.barrier()
            for cm in reversed(cms):
                cm.__exit__(None, None, None)
            return

        slab_tv = {}
        w2_tv = {}
        uitem = 0
        uF_base = self.p.cnt.get("uF", 0)
        self.p.sem("uF")
        for g in range(3):
            egrp = [3 * g, 3 * g + 1, 3 * g + 2]
            hready_v = None
            for ei, e_ in enumerate(egrp):
                self.pe("mm")
                if e_ < 8:
                    p.wait("tensor", "vC", p.cnt["vC"])
                    p.wait("tensor", "oF", p.cnt.get("oF", 0))
                    for h in range(2):
                        hsl = slice(h * TH, (h + 1) * TH)

                        def mmsb(e, e_=e_, h=h, hsl=hsl):
                            return e.matmul(self.bank(4 + h), self.c_E8[:, e_, :],
                                            shat_fm[0:8, hsl], start=True, stop=True)
                        p.run_sig("tensor", mmsb, "tC")
                    p.wait("vector", "tC", p.cnt["tC"])
                    for h in range(2):
                        hsl = slice(h * TH, (h + 1) * TH)
                        p.run("vector", lambda e, h=h, hsl=hsl: e.tensor_copy(
                            out=sbc[:, hsl], in_=self.bank(4 + h)))
                    p.sig("vector", "vC")
                    p.wait("tensor", "vC", p.cnt["vC"])
                for hc in range(HC):
                    si = e_ * HC + hc + call * NE * HC
                    slot = si % 2
                    if si >= 2 and (si - 2) in slab_tv:
                        p.wait("gpsimd", "tC", slab_tv[si - 2])
                    vw1 = p.dma("gpsimd", w13b[:, slot, 0, :, :], d["w1_t"][e_, hc], "dW")
                    vw3 = p.dma("gpsimd", w13b[:, slot, 1, :, :], d["w3_t"][e_, hc], "dW")
                    p.wait("tensor", "dW", vw3)
                    for h in range(2):
                        hsl = slice(h * TH, (h + 1) * TH)
                        pb = uitem % 2
                        if uitem >= 2:
                            p.wait("tensor", "uF", uF_base + uitem - 1)

                        def mmu(e, slot=slot, hsl=hsl, pb=pb):
                            for k in range(DC):
                                e.matmul(self.bank(0 + pb), w13b[:, slot, 0, k, :],
                                         self.lnb[:, k, hsl], start=(k == 0), stop=(k == DC - 1))
                            for k in range(DC):
                                r = e.matmul(self.bank(2 + pb), w13b[:, slot, 1, k, :],
                                             self.lnb[:, k, hsl], start=(k == 0), stop=(k == DC - 1))
                            return r
                        tvu = p.run_sig("tensor", mmu, "tC")
                        if h == 1:
                            slab_tv[si] = tvu
                        if e_ < 8:
                            p.wait("vector", "tC", tvu)
                            p.run_sig("vector", lambda e, pb=pb, hsl=hsl: e.tensor_mul(
                                m1s[:, 0, :], self.bank(0 + pb), sbc[:, hsl]), "vC")
                            p.wait("scalar", "vC", p.cnt["vC"])
                            p.run_sig("scalar", lambda e, pb=pb: e.activation(
                                out=h1s[:, 0, :], in_=m1s[:, 0, :], func=AF.Silu), "sC")
                            p.wait("vector", "sC", p.cnt["sC"])
                            p.run("vector", lambda e, pb=pb: e.tensor_mul(
                                h1s[:, 0, :], h1s[:, 0, :], self.bank(2 + pb)))
                            self.v_gap()
                            p.run("vector", lambda e, pb=pb, ei=ei, hc=hc, hsl=hsl: e.tensor_mul(
                                htil[:, ei, hc, hsl], h1s[:, 0, :], sbc[:, hsl]))
                        else:
                            p.wait("scalar", "tC", tvu)
                            p.run_sig("scalar", lambda e, pb=pb: e.activation(
                                out=h1s[:, 0, :], in_=self.bank(0 + pb), func=AF.Silu), "sC")
                            p.wait("vector", "sC", p.cnt["sC"])
                            p.run("vector", lambda e, pb=pb, ei=ei, hc=hc, hsl=hsl: e.tensor_mul(
                                htil[:, ei, hc, hsl], h1s[:, 0, :], self.bank(2 + pb)))
                        p.sig("vector", "uF")
                        uitem += 1
            hready_total = p.cnt["uF"]
            oitem = 0
            for oc in range(DC):
                wi = (call * 3 + g) * DC + oc
                slot = oc % 2
                if wi >= 2 and (wi - 2) in w2_tv:
                    p.wait("gpsimd", "tC", w2_tv[wi - 2])
                vws = [p.dma("gpsimd", w2b[:, slot, ei, :, :], d["w2_t"][e_g, oc], "dW2")
                       for ei, e_g in enumerate(egrp)]
                p.wait("tensor", "dW2", vws[-1])
                if oc == 0:
                    p.wait("tensor", "uF", hready_total)
                for h in range(2):
                    hsl = slice(h * TH, (h + 1) * TH)
                    pb = 4 + (oitem % 2)
                    if oitem >= 2:
                        p.wait("tensor", "oF", p.cnt["oF"] - 1)

                    def mmw2(e, slot=slot, hsl=hsl, pb=pb):
                        first = True
                        r = None
                        for ei in range(3):
                            for hc in range(HC):
                                r = e.matmul(self.bank(pb), w2b[:, slot, ei, hc, :],
                                             htil[:, ei, hc, hsl], start=first,
                                             stop=(ei == 2 and hc == HC - 1))
                                first = False
                        return r
                    tvw = p.run_sig("tensor", mmw2, "tC")
                    if h == 1:
                        w2_tv[wi] = tvw
                    p.wait("vector", "tC", tvw)
                    vv = p.run_sig("vector", lambda e, oc=oc, hsl=hsl, pb=pb: e.scalar_tensor_tensor(
                        out=X[:, oc, hsl], in0=self.bank(pb), scalar=0.5, in1=X[:, oc, hsl],
                        op0=ALU.mult, op1=ALU.add), "oF")
                    oitem += 1
                    if final and g == 2:
                        p.wait("sync", "oF", vv)
                        p.dma("sync", d["out_fm"][oc * 128:(oc + 1) * 128, hsl], X[:, oc, hsl], "dOut")
        p.barrier()
        for cm in reversed(cms):
            cm.__exit__(None, None, None)

    def attn(self):
        p = self.p
        d = self.d
        nc = self.nc
        p.barrier()
        p.dma("sync", d["x2_sp"].rearrange("c p t -> p c t"), self.X[:, :, :], "dS")
        p.wait("sync", "dS", p.cnt["dS"])
        p.barrier()
        self.free_x()
        cms_keep = []
        cms1 = []
        cms = cms1

        def al(name, shape, dt):
            cm = nc.sbuf_tensor(name, shape, dt)
            h = cm.__enter__()
            cms.append(cm)
            return h
        cms = cms_keep
        wexp = al("wexp", [128, NCH, 4, 32], F32)
        cms = cms1
        Prun = al("Prun", [128, NH, 128], F32)
        Brun = al("Brun", [128, NH, 128], F32)
        GtAll = al("GtAll", [128, NCH, 96], F32)
        g_fm = al("g_fm", [32, T], F32)
        lnB_fm = al("lnB_fm", [32, T], F32)
        ba_fm = al("ba_fm", [64, T], F32)
        bafm2 = ba_fm
        batm = al("batm", [128, 64], F32)
        Gpack = al("Gpack", [96, 128], F32)
        cum = al("cum", [32, 2, 128], F32)
        convpp = al("convpp", [128, MIX_NC, 4], F32)
        qknorm = al("qknorm", [128, 32, TH], BF16)
        pool = al("pool", [128, 4, 515], BF16)
        tails = al("tails", [128, MIX_NC, 3], BF16)
        stash012 = al("stash012", [128, MIX_NC, 3], BF16)
        zstage = al("zstage", [128, 2, TH], BF16)
        vstage = al("vstage", [128, 1, TH], BF16)
        mixed3 = al("mixed3", [128, MIX_NC, 3], F32)
        halo = al("halo", [128, MIX_NC, 3], F32)
        haloscr = al("haloscr", [128, MIX_NC, 3], F32)
        patch = mixed3
        ptmp = haloscr
        patchb = al("patchb", [128, MIX_NC, 3], BF16)
        sq3 = al("sq3", [128, 32, 3], BF16)
        postq = al("postq", [128, TH], BF16)
        sqt = al("sqt", [128, TH], BF16)
        rinvt = al("rinvt", [128, TH], F32)
        wqb = al("wqb", [128, 2, DC, 128], BF16)
        wbab = al("wbab", [128, DC, 64], BF16)
        convb = al("convb", [128, 2, 2, 4, 128], BF16)
        DT_U = al("DT_U", [128, HGRP, 128], BF16)
        DT_L = al("DT_L", [128, HGRP, 128], BF16)
        DT_I = al("DT_I", [128, HGRP, 128], BF16)
        B0a = al("B0a", [128, HGRP, 128], F32)
        B0b = al("B0b", [128, HGRP, 128], F32)
        tileb = al("tileb", [128, HGRP, 128], F32)
        tileb2 = al("tileb2", [128, HGRP, 128], F32)
        Ptk = al("Ptk", [128, HGRP, 128], BF16)
        Ttk = al("Ttk", [128, HGRP, 128], BF16)
        Ptk2 = al("Ptk2", [128, HGRP, 128], BF16)
        Ttk2 = al("Ttk2", [128, HGRP, 128], BF16)
        Rall = al("Rall", [128, HGRP, 256], BF16)
        Ktm = al("Ktm", [128, NKH, 128], BF16)
        vrel = al("vrel", [128, NH, 128], BF16)
        mstage = al("mstage", [128, HGRP, 128], BF16)
        wstage = al("wstage", [128, HGRP, 128], BF16)
        kpt = al("kpt", [128, HGRP, 128], BF16)
        PcT = B0a
        DSc = tileb
        expGt = al("expGt", [128, 32], F32)
        wtmp = al("wtmp", [128, 32], F32)

        p.dma("gpsimd", wbab[:, :, :], d["wba_t"][:, :, :], "dW")
        p.dma("gpsimd", convpp[:, :, :], d["conv_pp"][:, :, :], "dW")
        p.wait("tensor", "dW", p.cnt["dW"])
        p.wait("tensor", "vC", p.cnt["vC"])
        for h in range(2):
            hsl = slice(h * TH, (h + 1) * TH)

            def mmba(e, hsl=hsl):
                for k in range(DC):
                    r = e.matmul(self.bank(5)[0:64, :], wbab[:, k, :], self.lnb[:, k, hsl],
                                 start=(k == 0), stop=(k == DC - 1))
                return r
            p.run_sig("tensor", mmba, "tC")
            p.wait("vector", "tC", p.cnt["tC"])
            p.run_sig("vector", lambda e, hsl=hsl: e.tensor_copy(
                out=ba_fm[:, hsl], in_=self.bank(5)[0:64, :]), "vC")
            p.wait("tensor", "vC", p.cnt["vC"])
        for c in range(NCH):
            csl = slice(c * C, (c + 1) * C)
            p.wait("tensor", "vC", p.cnt["vC"])

            def mmt1(e, csl=csl):
                return e.matmul(self.bank(5)[:, 0:64], ba_fm[:, csl],
                                self.c_identf[0:64, 0:64], start=True, stop=True)
            p.run_sig("tensor", mmt1, "tC")
            p.wait("vector", "tC", p.cnt["tC"])
            bsl = self.bank(5)[:, 0:64].rearrange("p (kh j) -> p kh j", j=4)
            p.run("vector", lambda e, bsl=bsl: e.tensor_copy(
                out=batm[:, 0:32].rearrange("p (kh j) -> p kh j", j=2), in_=bsl[:, :, 0:2]))
            p.run_sig("vector", lambda e, bsl=bsl: e.tensor_copy(
                out=batm[:, 32:64].rearrange("p (kh j) -> p kh j", j=2), in_=bsl[:, :, 2:4]), "vC")
            p.wait("tensor", "vC", p.cnt["vC"])

            def mmt2(e):
                return e.matmul(self.bank(4)[0:64, 0:128], batm[:, :],
                                self.c_identf[:, :], start=True, stop=True)
            p.run_sig("tensor", mmt2, "tC")
            p.wait("vector", "tC", p.cnt["tC"])
            p.run_sig("vector", lambda e, csl=csl: e.tensor_copy(
                out=bafm2[:, csl], in_=self.bank(4)[0:64, 0:128]), "vC")
        p.wait("tensor", "vC", p.cnt["vC"])
        for h in range(2):
            hsl = slice(h * TH, (h + 1) * TH)
            p.wait("scalar", "vC", p.cnt["vC"])
            p.run_sig("scalar", lambda e, hsl=hsl: e.activation(
                out=lnB_fm[:, hsl], in_=bafm2[0:32, hsl], func=AF.Sigmoid), "sC")
        for h in range(2):
            hsl = slice(h * TH, (h + 1) * TH)
            p.run("scalar", lambda e, hsl=hsl: e.activation(
                out=lnB_fm[:, hsl], in_=lnB_fm[:, hsl], func=AF.Ln))
            p.run("scalar", lambda e, hsl=hsl: e.activation(
                out=g_fm[:, hsl], in_=bafm2[32:64, hsl], func=AF.Exp, bias=self.c_dtcol[:, :]))
            p.run_sig("scalar", lambda e, hsl=hsl: e.activation(
                out=g_fm[:, hsl], in_=g_fm[:, hsl], func=AF.Ln,
                bias=self.c_cols[0:32, 1:2]), "sC")
            p.wait("vector", "sC", p.cnt["sC"])
            p.run_sig("vector", lambda e, hsl=hsl: e.tensor_scalar(
                out=g_fm[:, hsl], in0=g_fm[:, hsl], scalar1=self.c_negexpA[:, :],
                scalar2=None, op0=ALU.mult), "vC")
        for c in range(NCH):
            csl = slice(c * C, (c + 1) * C)
            p.run("vector", lambda e, csl=csl: e.tensor_copy(out=cum[:, 0, :], in_=g_fm[:, csl]))
            cur, s_ = 0, 1
            while s_ < C:
                nxt = 1 - cur
                self.v_gap()
                p.run("vector", lambda e, cur=cur, nxt=nxt, s_=s_: e.tensor_copy(
                    out=cum[:, nxt, 0:s_], in_=cum[:, cur, 0:s_]))
                p.run("vector", lambda e, cur=cur, nxt=nxt, s_=s_: e.tensor_add(
                    cum[:, nxt, s_:C], cum[:, cur, s_:C], cum[:, cur, 0:C - s_]))
                cur, s_ = nxt, s_ * 2
            self.v_gap()
            p.run("vector", lambda e, cur=cur: e.tensor_copy(out=Gpack[0:32, :], in_=cum[:, cur, :]))
            p.run("vector", lambda e, cur=cur, csl=csl: e.tensor_add(
                Gpack[32:64, :], cum[:, cur, :], lnB_fm[:, csl]))
            self.v_gap()
            p.run_sig("vector", lambda e, cur=cur: e.tensor_scalar(
                out=Gpack[64:96, :], in0=self.c_onesf[0:32, :], scalar1=cum[:, cur, 127:128],
                scalar2=None, op0=ALU.mult), "vC")
            p.wait("tensor", "vC", p.cnt["vC"])

            def mmgp(e):
                return e.matmul(self.bank(5)[:, 0:96], Gpack[:, :],
                                self.c_identf[0:96, 0:96], start=True, stop=True)
            p.run_sig("tensor", mmgp, "tC")
            p.wait("vector", "tC", p.cnt["tC"])
            p.run_sig("vector", lambda e, c=c: e.tensor_copy(
                out=GtAll[:, c, :], in_=self.bank(5)[:, 0:96]), "vC")
            self.v_gap()
            p.run("vector", lambda e, c=c: e.tensor_sub(
                wtmp[:, 0:32], GtAll[:, c, 64:96], GtAll[:, c, 0:32]))
            p.run_sig("vector", lambda e, c=c: e.tensor_sub(
                wexp[:, c, 1, :], GtAll[:, c, 32:64], GtAll[:, c, 0:32]), "vC")
            p.wait("scalar", "vC", p.cnt["vC"])
            p.run("scalar", lambda e, c=c: e.activation(out=wexp[:, c, 0, :], in_=wtmp[:, 0:32], func=AF.Exp))
            p.run("scalar", lambda e, c=c: e.activation(out=wexp[:, c, 1, :], in_=wexp[:, c, 1, :], func=AF.Exp))
            p.run("scalar", lambda e, c=c: e.activation(out=wexp[:, c, 2, :], in_=GtAll[:, c, 32:64], func=AF.Exp))
            p.run_sig("scalar", lambda e, c=c: e.activation(out=wexp[:, c, 3, :], in_=GtAll[:, c, 64:96], func=AF.Exp), "sC")
        p.wait("vector", "sC", p.cnt["sC"])
        p.run_sig("vector", lambda e: e.memset(tails[:, :, :], 0.0), "vC")

        st = {"qk_tv": {}, "si": 0, "zi": 0, "z_dma": {}, "ci": 0, "conv_tv": {},
              "pool_user": {}, "conv_bank_ev": {}, "ev": {}, "first_chunk": True}

        def emit_a1_half(h):
            hsl = slice(h * TH, (h + 1) * TH)
            if h == 1:
                p.wait("tensor", "sC", p.cnt["sC"])
                p.wait("tensor", "vC", p.cnt["vC"])
                p.wait("vector", "dS", p.cnt["dS"])
                p.wait("scalar", "dS", p.cnt["dS"])
            for oc in range(QKVZ_OC):
                kh, j = oc // 6, oc % 6
                si = st["si"]
                st["si"] += 1
                slot = si % 2
                if (si - 2) in st["qk_tv"]:
                    p.wait("gpsimd", "tC", st["qk_tv"][si - 2])
                vwq = p.dma("gpsimd", wqb[:, slot, :, :], d["wq_t"][oc], "dW")
                p.wait("tensor", "dW", vwq)
                bq = si % 2
                if (si - 2) in st["ev"]:
                    p.wait("tensor", "sC", st["ev"][si - 2])

                def mmq(e, slot=slot, hsl=hsl, bq=bq):
                    for k in range(DC):
                        r = e.matmul(self.bank(bq), wqb[:, slot, k, :], self.lnb[:, k, hsl],
                                     start=(k == 0), stop=(k == DC - 1))
                    return r
                tv = p.run_sig("tensor", mmq, "tC")
                if h == 0 and j < 4:
                    mc3 = (kh if j == 0 else 16 + kh if j == 1 else 32 + 2 * kh + (j - 2))

                    def mmq3(e, slot=slot, mc3=mc3):
                        for k in range(DC):
                            r = e.matmul(self.bank(4)[:, mc3 * 3:mc3 * 3 + 3], wqb[:, slot, k, :],
                                         self.lnb[:, k, T - 3:T], start=(k == 0), stop=(k == DC - 1))
                        return r
                    tv = p.run_sig("tensor", mmq3, "tC")
                st["qk_tv"][si] = tv
                if j >= 4:
                    zc = 2 * kh + (j - 4)
                    zi = st["zi"]
                    st["zi"] += 1
                    zsl = zi % 2
                    if (zi - 2) in st["z_dma"]:
                        p.wait("scalar", "dS", st["z_dma"][zi - 2])
                    p.wait("scalar", "tC", tv)
                    ev = p.run_sig("scalar", lambda e, bq=bq, zsl=zsl: e.copy(
                        out=zstage[:, zsl, :], in_=self.bank(bq)), "sC")
                    st["ev"][si] = ev
                    p.wait("sync", "sC", ev)
                    st["z_dma"][zi] = p.dma(
                        "sync", d["z_sp"][h * 4:(h + 1) * 4, zc].rearrange("c p t -> p c t"),
                        zstage[:, zsl, :].rearrange("p (c t) -> p c t", t=128), "dS")
                    continue
                mc = (kh if j == 0 else 16 + kh if j == 1 else 32 + 2 * kh + (j - 2))
                pslot = mc % 4
                if pslot in st["pool_user"]:
                    tvp, vvp = st["pool_user"][pslot]
                    p.wait("scalar", "tC", tvp)
                    p.wait("scalar", "vC", vvp)
                p.wait("scalar", "tC", tv)
                ev = p.run_sig("scalar", lambda e, bq=bq, pslot=pslot: e.copy(
                    out=pool[:, pslot, 3:515], in_=self.bank(bq)), "sC")
                st["ev"][si] = ev
                ev2 = p.run_sig("scalar", lambda e, pslot=pslot, mc=mc: e.copy(
                    out=pool[:, pslot, 0:3], in_=tails[:, mc, :]), "sC")
                p.wait("vector", "sC", ev2)
                p.run("vector", lambda e, pslot=pslot, mc=mc: e.tensor_copy(
                    out=tails[:, mc, :], in_=pool[:, pslot, 512:515]))
                vvs = p.run_sig("vector", lambda e, pslot=pslot, mc=mc: e.tensor_copy(
                    out=stash012[:, mc, :], in_=pool[:, pslot, 3:6]), "vC")
                ci = st["ci"]
                st["ci"] += 1
                slab = ci // 2
                cslot = slab % 2
                if ci % 2 == 0:
                    if (slab - 2) in st["conv_tv"]:
                        p.wait("gpsimd", "tC", st["conv_tv"][slab - 2])
                    dsl = (slab % 32) * 2
                    st["conv_dma"] = p.dma("gpsimd", convb[:, cslot, :, :, :],
                                           d["conv_diag"][dsl:dsl + 2].rearrange("s p j c -> p s j c"), "dW")
                p.wait("tensor", "dW", st["conv_dma"])
                p.wait("tensor", "vC", vvs)
                bc = 2 + ci % 2
                if bc in st["conv_bank_ev"]:
                    p.wait("tensor", "sC", st["conv_bank_ev"][bc])

                def mmc(e, cslot=cslot, pslot=pslot, bc=bc, ci=ci):
                    for jj in range(4):
                        r = e.matmul(self.bank(bc), convb[:, cslot, ci % 2, jj, :],
                                     pool[:, pslot, jj:jj + 512], start=(jj == 0), stop=(jj == 3))
                    return r
                tvc = p.run_sig("tensor", mmc, "tC")
                if ci % 2 == 1:
                    st["conv_tv"][ci // 2] = tvc
                st["pool_user"][pslot] = (tvc, vvs)
                p.wait("scalar", "tC", tvc)
                if mc >= 32:
                    p.wait("scalar", "dS", st.get("v_dma", 0))
                    ev3 = p.run_sig("scalar", lambda e, bc=bc: e.activation(
                        out=vstage[:, 0, :], in_=self.bank(bc), func=AF.Silu), "sC")
                    st["conv_bank_ev"][bc] = ev3
                    p.wait("sync", "sC", ev3)
                    st["v_dma"] = p.dma("sync", d["v_sp"][mc - 32, :, hsl], vstage[:, 0, :], "dS")
                else:
                    p.run("scalar", lambda e, bc=bc: e.activation(
                        out=postq[:, :], in_=self.bank(bc), func=AF.Silu))
                    ev3 = p.run_sig("scalar", lambda e: e.activation(
                        out=sqt[:, :], in_=postq[:, :], func=AF.Square), "sC")
                    st["conv_bank_ev"][bc] = ev3
                    p.wait("tensor", "sC", ev3)

                    def mml2(e):
                        return e.matmul(self.bank(5), self.c_onesb[:, :], sqt[:, :],
                                        start=True, stop=True)
                    tvl = p.run_sig("tensor", mml2, "tC")
                    p.wait("scalar", "tC", tvl)
                    scale = float(DK) if mc < 16 else 1.0
                    bcol = self.c_cols[:, 2:3] if mc < 16 else self.c_cols[:, 3:4]
                    p.run_sig("scalar", lambda e, scale=scale, bcol=bcol: e.activation(
                        out=rinvt[:, :], in_=self.bank(5), func=AF.Sqrt, scale=scale, bias=bcol), "sC")
                    p.wait("vector", "sC", p.cnt["sC"])
                    p.run("vector", lambda e: e.reciprocal(out=rinvt[:, :], in_=rinvt[:, :]))
                    self.v_gap()
                    p.run_sig("vector", lambda e, mc=mc: e.tensor_mul(
                        qknorm[:, mc, :], postq[:, :], rinvt[:, :]), "vC")
                    p.wait("tensor", "vC", p.cnt["vC"])
            if h == 0:
                p.wait("vector", "tC", p.cnt["tC"])
                p.run_sig("vector", lambda e: e.tensor_copy(
                    out=mixed3[:, :, :],
                    in_=self.bank(4)[:, 0:192].rearrange("p (m j) -> p m j", j=3)), "vC")
                p.wait("sync", "vC", p.cnt["vC"])
                p.dma("sync", d["ag1_in"].rearrange("m p j -> p m j"), mixed3[:, :, :], "dS")
                p.wait("gpsimd", "dS", p.cnt["dS"])
                p.run_sig("gpsimd", lambda e: e.collective_compute(
                    "AllGather", ALU.bypass, replica_groups=[list(range(NCORES))],
                    ins=[d["ag1_in"].opt()], outs=[d["ag1_out"].opt()]), "cc")
                p.wait("sync", "cc", p.cnt["cc"])
                for dd in range(NCORES):
                    vh = p.dma("sync", haloscr[:, :, :], d["ag1_out"][dd].rearrange("m p j -> p m j"), "dS")
                    p.wait("vector", "dS", vh)
                    if dd == 0:
                        p.run("vector", lambda e: e.tensor_scalar(
                            out=halo[:, :, :], in0=haloscr[:, :, :], scalar1=self.c_ohp[:, 0:1],
                            scalar2=None, op0=ALU.mult))
                    else:
                        p.run("vector", lambda e, dd=dd: e.scalar_tensor_tensor(
                            out=halo[:, :, :], in0=haloscr[:, :, :], scalar=self.c_ohp[:, dd:dd + 1],
                            in1=halo[:, :, :], op0=ALU.mult, op1=ALU.add))
                    vv = p.sig("vector", "vC")
                    p.wait("sync", "vC", vv)
                for t_ in range(3):
                    for jj in range(4):
                        w_ = t_ + jj
                        src_ = halo[:, :, w_:w_ + 1] if w_ < 3 else stash012[:, :, w_ - 3:w_ - 2]
                        if jj == 0:
                            p.run("vector", lambda e, t_=t_, src_=src_: e.tensor_mul(
                                patch[:, :, t_:t_ + 1], src_, convpp[:, :, 0:1]))
                        else:
                            p.run("vector", lambda e, t_=t_, jj=jj, src_=src_: e.tensor_mul(
                                ptmp[:, :, t_:t_ + 1], src_, convpp[:, :, jj:jj + 1]))
                            self.v_gap()
                            p.run("vector", lambda e, t_=t_: e.tensor_add(
                                patch[:, :, t_:t_ + 1], patch[:, :, t_:t_ + 1], ptmp[:, :, t_:t_ + 1]))
                            self.v_gap()
                vvp = p.sig("vector", "vC")
                p.wait("scalar", "vC", vvp)
                p.run("scalar", lambda e: e.activation(out=patchb[:, :, :], in_=patch[:, :, :], func=AF.Silu))
                ev4 = p.run_sig("scalar", lambda e: e.activation(
                    out=sq3[:, :, :], in_=patchb[:, 0:32, :], func=AF.Square), "sC")
                p.wait("tensor", "sC", ev4)

                def mml3(e):
                    for mcq in range(32):
                        r = e.matmul(self.bank(5)[:, mcq * 3:mcq * 3 + 3], self.c_onesb[:, :],
                                     sq3[:, mcq, :], start=True, stop=True)
                    return r
                tvq = p.run_sig("tensor", mml3, "tC")
                p.wait("scalar", "tC", tvq)
                p.run("scalar", lambda e: e.activation(
                    out=rinvt[:, 0:48], in_=self.bank(5)[:, 0:48], func=AF.Sqrt,
                    scale=float(DK), bias=self.c_cols[:, 2:3]))
                ev5 = p.run_sig("scalar", lambda e: e.activation(
                    out=rinvt[:, 48:96], in_=self.bank(5)[:, 48:96], func=AF.Sqrt,
                    scale=1.0, bias=self.c_cols[:, 3:4]), "sC")
                p.wait("vector", "sC", ev5)
                p.run("vector", lambda e: e.reciprocal(out=rinvt[:, 0:96], in_=rinvt[:, 0:96]))
                self.v_gap()
                p.run_sig("vector", lambda e: e.tensor_mul(
                    qknorm[:, 0:32, 0:3],
                    patchb[:, 0:32, :],
                    rinvt[:, 0:96].rearrange("p (m j) -> p m j", j=3)), "vC")
                p.wait("tensor", "vC", p.cnt["vC"])
            p.wait("sync", "vC", p.cnt["vC"])
            p.dma("sync", d["q_sp"][:, :, hsl].rearrange("m p t -> p m t"),
                  qknorm[:, 0:16, :], "dS")
            st["q_spill"] = p.cnt["dS"]

        def emit_a2_chunk(c):
            h = c // 4
            csl = slice((c % 4) * 128, (c % 4) * 128 + 128)
            vv_ = p.dma("scalar", vrel[:, :, :],
                        d["v_sp"][:, :, c * C:(c + 1) * C].rearrange("m p t -> p m t"), "dA")
            if c == 0:
                p.wait("vector", "dA", vv_)
                p.run_sig("vector", lambda e: e.tensor_copy(
                    out=vrel[:, :, 0:3], in_=patchb[:, 32:64, :]), "vC")
            p.wait("tensor", "dA", vv_)
            p.wait("tensor", "vC", p.cnt["vC"])
            p.wait("tensor", "sC", p.cnt["sC"])

            def mmktm(e, csl=csl):
                for kh in range(NKH):
                    r = e.matmul(self.bank(kh // 4)[:, (kh % 4) * 128:(kh % 4) * 128 + 128],
                                 qknorm[:, 16 + kh, csl], self.c_identb[:, :],
                                 start=True, stop=True)
                return r
            tvk = p.run_sig("tensor", mmktm, "tC")
            p.wait("vector", "tC", tvk)
            for b_ in range(4):
                p.run_sig("vector", lambda e, b_=b_: e.tensor_copy(
                    out=Ktm[:, b_ * 4:b_ * 4 + 4, :], in_=self.bank(b_)), "vC")
            p.wait("sync", "vC", p.cnt["vC"])
            p.dma("sync", d["ktm_sp"][c].rearrange("m p t -> p m t"), Ktm[:, :, :], "dS")
            p.wait("tensor", "vC", p.cnt["vC"])
            p.wait("scalar", "vC", p.cnt["vC"])
            p.run_sig("scalar", lambda e, c=c: e.activation(
                out=expGt[:, :], in_=GtAll[:, c, 0:32], func=AF.Exp), "sC")
            p.wait("vector", "sC", p.cnt["sC"])
            for g in range(NGRP):
                h0 = g * HGRP
                batches = (("U", DT_U), ("L", DT_L), ("I", DT_I), ("W", None))
                tbs = (tileb, tileb2)
                tvd_hist = {}
                vve_hist = {}
                exp_hist = {}

                def emit_build(bi):
                    name = batches[bi][0]
                    tb = tbs[bi % 2]
                    if bi >= 2:
                        p.wait("vector", "tC", tvd_hist[bi - 2])
                        p.wait("vector", "sC", exp_hist.get(bi - 2, 0))
                    for hh in range(HGRP):
                        hq = h0 + hh
                        if name == "U":
                            fn = lambda e, hh=hh, hq=hq, c=c, tb=tb: e.tensor_scalar(
                                out=tb[:, hh, :], in0=self.c_mU[:, :],
                                scalar1=GtAll[:, c, 32 + hq:33 + hq], scalar2=None, op0=ALU.add)
                        elif name == "L":
                            fn = lambda e, hh=hh, hq=hq, c=c, tb=tb: e.tensor_scalar(
                                out=tb[:, hh, :], in0=self.c_mL[:, :],
                                scalar1=GtAll[:, c, hq:hq + 1], scalar2=None, op0=ALU.subtract)
                        elif name == "I":
                            fn = lambda e, hh=hh, hq=hq, c=c, tb=tb: e.tensor_scalar(
                                out=tb[:, hh, :], in0=self.c_mI[:, :],
                                scalar1=GtAll[:, c, hq:hq + 1], scalar2=None, op0=ALU.add)
                        else:
                            fn = lambda e, hh=hh, hq=hq, tb=tb: e.tensor_scalar(
                                out=tb[:, hh, :], in0=self.c_onesf[:, :],
                                scalar1=expGt[:, hq:hq + 1], scalar2=None, op0=ALU.mult)
                        p.run("vector", fn)
                emit_build(0)
                for bi, (name, dst) in enumerate(batches):
                    tb = tbs[bi % 2]
                    pb0 = 4 + 2 * (bi % 2)
                    tsem, tval = p.token("vector")
                    p.wait("tensor", tsem, tval)
                    if bi >= 2:
                        p.wait("tensor", "vC", vve_hist[bi - 2])

                    def mmd(e, tb=tb, pb0=pb0):
                        for hh in range(HGRP):
                            bnk = pb0 + hh // 4
                            r = e.matmul(self.bank(bnk)[:, (hh % 4) * 128:(hh % 4) * 128 + 128],
                                         tb[:, hh, :], self.c_identf[:, :], start=True, stop=True)
                        return r
                    tvd = p.run_sig("tensor", mmd, "tC")
                    tvd_hist[bi] = tvd
                    if bi + 1 < 4:
                        emit_build(bi + 1)
                    p.wait("vector", "tC", tvd)
                    if name == "U":
                        for hh in range(HGRP):
                            p.run("vector", lambda e, hh=hh, hq=h0 + hh, c=c: e.tensor_scalar(
                                out=B0a[:, hh, :], in0=self.c_onesf[:, :],
                                scalar1=GtAll[:, c, hq:hq + 1], scalar2=None, op0=ALU.mult))
                        self.v_gap()
                        for b_ in range(2):
                            p.run("vector", lambda e, b_=b_, tb=tb, pb0=pb0: e.tensor_sub(
                                tb[:, b_ * 4:b_ * 4 + 4, :], self.bank(pb0 + b_), B0a[:, b_ * 4:b_ * 4 + 4, :]))
                    elif name == "L":
                        for hh in range(HGRP):
                            p.run("vector", lambda e, hh=hh, hq=h0 + hh, c=c: e.tensor_scalar(
                                out=B0b[:, hh, :], in0=self.c_onesf[:, :],
                                scalar1=GtAll[:, c, 32 + hq:33 + hq], scalar2=None, op0=ALU.mult))
                        self.v_gap()
                        for b_ in range(2):
                            p.run("vector", lambda e, b_=b_, tb=tb, pb0=pb0: e.tensor_add(
                                tb[:, b_ * 4:b_ * 4 + 4, :], self.bank(pb0 + b_), B0b[:, b_ * 4:b_ * 4 + 4, :]))
                    elif name == "I":
                        for b_ in range(2):
                            p.run("vector", lambda e, b_=b_, tb=tb, pb0=pb0: e.tensor_sub(
                                tb[:, b_ * 4:b_ * 4 + 4, :], self.bank(pb0 + b_), B0a[:, b_ * 4:b_ * 4 + 4, :]))
                    else:
                        for b_ in range(2):
                            p.run("vector", lambda e, b_=b_, pb0=pb0: e.tensor_copy(
                                out=wstage[:, b_ * 4:b_ * 4 + 4, :], in_=self.bank(pb0 + b_)))
                    vve = p.sig("vector", "vC")
                    vve_hist[bi] = vve
                    if name != "W":
                        p.wait("scalar", "vC", vve)
                        ex = p.run_sig("scalar", lambda e, dst=dst, tb=tb: e.activation(
                            out=dst[:, :, :], in_=tb[:, :, :], func=AF.Exp), "sC")
                        exp_hist[bi] = ex
                p.wait("sync", "vC", p.cnt["vC"])
                st["wbc_dma"] = p.dma("sync", d["wbc_sp"][c, h0:h0 + HGRP].rearrange("m p t -> p m t"),
                                       wstage[:, :, :], "dS")
                p.wait("tensor", "sC", p.cnt["sC"])
                p.wait("tensor", "vC", p.cnt["vC"])
                p.wait("vector", "sC", p.cnt["sC"])

                def mmkk(e, csl=csl, g=g):
                    for i4 in range(4):
                        kh = g * 4 + i4
                        for r_ in range(2):
                            col = (i4 * 2 + r_) * 128
                            e.matmul(self.bank(4 + col // 512)[:, col % 512:col % 512 + 128],
                                     qknorm[:, 16 + kh, csl], qknorm[:, 16 + kh, csl],
                                     start=True, stop=True)
                    for i4 in range(4):
                        kh = g * 4 + i4
                        for r_ in range(2):
                            col = (i4 * 2 + r_) * 128
                            r = e.matmul(self.bank(6 + col // 512)[:, col % 512:col % 512 + 128],
                                         qknorm[:, 16 + kh, csl], qknorm[:, kh, csl],
                                         start=True, stop=True)
                    return r
                tvkk = p.run_sig("tensor", mmkk, "tC")
                p.wait("vector", "tC", tvkk)
                p.wait("vector", "dS", st.get("mt_dma", 0))
                for b_ in range(2):
                    p.run("vector", lambda e, b_=b_: e.tensor_mul(
                        Ttk[:, b_ * 4:b_ * 4 + 4, :], self.bank(4 + b_), DT_U[:, b_ * 4:b_ * 4 + 4, :]))
                for b_ in range(2):
                    p.run("vector", lambda e, b_=b_: e.tensor_mul(
                        Ptk[:, b_ * 4:b_ * 4 + 4, :], self.bank(4 + b_), DT_L[:, b_ * 4:b_ * 4 + 4, :]))
                for b_ in range(2):
                    p.run("vector", lambda e, b_=b_: e.tensor_mul(
                        mstage[:, b_ * 4:b_ * 4 + 4, :], self.bank(6 + b_), DT_I[:, b_ * 4:b_ * 4 + 4, :]))
                vvm = p.sig("vector", "vC")
                p.wait("sync", "vC", vvm)
                p.dma("sync", d["mt_sp"][c, h0:h0 + HGRP].rearrange("m p t -> p m t"),
                      mstage[:, :, :], "dS")
                st["mt_dma"] = p.cnt["dS"]
                p.wait("tensor", "vC", vvm)
                def mmvt(e, g=g):
                    for hh in range(HGRP):
                        bnk = 4 + hh // 4
                        r = e.matmul(self.bank(bnk)[:, (hh % 4) * 128:(hh % 4) * 128 + 128],
                                     vrel[:, g * HGRP + hh, :], self.c_identb[:, :],
                                     start=True, stop=True)
                    return r
                tvv = p.run_sig("tensor", mmvt, "tC")
                p.wait("vector", "tC", tvv)
                for hh in range(HGRP):
                    hq = h0 + hh
                    bnk = 4 + hh // 4
                    p.run("vector", lambda e, hh=hh, hq=hq, bnk=bnk, c=c: e.tensor_scalar(
                        out=Rall[:, hh, 0:128], in0=self.bank(bnk)[:, (hh % 4) * 128:(hh % 4) * 128 + 128],
                        scalar1=wexp[:, c, 1, hq:hq + 1], scalar2=None, op0=ALU.mult))
                self.v_gap()
                p.run("vector", lambda e: e.tensor_scalar(
                    out=Rall[:, :, 0:128], in0=Rall[:, :, 0:128], scalar1=-1.0, scalar2=None, op0=ALU.mult))
                for hh in range(HGRP):
                    hq = h0 + hh
                    kh = hq // 2
                    p.run("vector", lambda e, hh=hh, hq=hq, kh=kh, c=c: e.tensor_scalar(
                        out=Rall[:, hh, 128:256], in0=Ktm[:, kh, :],
                        scalar1=wexp[:, c, 2, hq:hq + 1], scalar2=None, op0=ALU.mult))
                vvr = p.sig("vector", "vC")
                p.wait("tensor", "vC", vvr)
                for lev in range(NLEV):
                    Ps, Ts = (Ptk, Ttk) if lev % 2 == 0 else (Ptk2, Ttk2)
                    Pd, Td = (Ptk2, Ttk2) if lev % 2 == 0 else (Ptk, Ttk)
                    iden = (self.c_negidentb if lev == 0 else self.c_identb)[:, :]

                    def mmapp(e, Ts=Ts, iden=iden):
                        for hh in range(HGRP):
                            bnk = hh // 2
                            col = (hh % 2) * 256
                            e.matmul(self.bank(bnk)[:, col:col + 256], iden, Rall[:, hh, :],
                                     start=True, stop=False)
                            r = e.matmul(self.bank(bnk)[:, col:col + 256], Ts[:, hh, :], Rall[:, hh, :],
                                         start=False, stop=True)
                        return r
                    tva = p.run_sig("tensor", mmapp, "tC")
                    if lev < NLEV - 1:
                        def mmsq(e, Ps=Ps, Ts=Ts):
                            for hh in range(HGRP):
                                bnk = 4 + hh // 2
                                col = (hh % 2) * 256
                                e.matmul(self.bank(bnk)[:, col:col + 128], Ts[:, hh, :], Ps[:, hh, :],
                                         start=True, stop=True)
                                r = e.matmul(self.bank(bnk)[:, col + 128:col + 256], Ps[:, hh, :], Ts[:, hh, :],
                                             start=True, stop=True)
                            return r
                        tvs = p.run_sig("tensor", mmsq, "tC")
                    p.wait("scalar", "tC", tva)
                    for bnk in range(4):
                        p.run("scalar", lambda e, bnk=bnk: e.copy(
                            out=Rall[:, bnk * 2:bnk * 2 + 2, :], in_=self.bank(bnk)))
                    sva = p.sig("scalar", "sC")
                    if lev < NLEV - 1:
                        p.wait("vector", "tC", tvs)
                        for q_ in range(4):
                            bk = self.psA[:, 4 + q_, :].rearrange("p (h x) -> p h x", x=256)
                            p.run("vector", lambda e, q_=q_, bk=bk, Pd=Pd: e.tensor_copy(
                                out=Pd[:, q_ * 2:q_ * 2 + 2, :], in_=bk[:, :, 0:128]))
                            p.run("vector", lambda e, q_=q_, bk=bk, Td=Td: e.tensor_copy(
                                out=Td[:, q_ * 2:q_ * 2 + 2, :], in_=bk[:, :, 128:256]))
                        svv = p.sig("vector", "vC")
                        p.wait("tensor", "sC", sva)
                        p.wait("tensor", "vC", svv)
                    else:
                        p.wait("tensor", "sC", sva)
                p.wait("sync", "sC", p.cnt["sC"])
                p.dma("sync", d["ua_sp"][c, h0:h0 + HGRP].rearrange("m p t -> p m t"),
                      Rall[:, :, 0:128], "dS")
                def mmwt(e):
                    for hh in range(HGRP):
                        bnk = 4 + hh // 4
                        r = e.matmul(self.bank(bnk)[:, (hh % 4) * 128:(hh % 4) * 128 + 128],
                                     Rall[:, hh, 128:256], self.c_identb[:, :], start=True, stop=True)
                    return r
                tvw = p.run_sig("tensor", mmwt, "tC")
                p.wait("vector", "tC", tvw)
                p.wait("vector", "dS", st.get("wbc_dma", 0))
                for b_ in range(2):
                    p.run("vector", lambda e, b_=b_: e.tensor_copy(
                        out=wstage[:, b_ * 4:b_ * 4 + 4, :], in_=self.bank(4 + b_)))
                vvw = p.sig("vector", "vC")
                p.wait("sync", "vC", vvw)
                p.dma("sync", d["mw_sp"][c, h0:h0 + HGRP].rearrange("m p t -> p m t"),
                      wstage[:, :, :], "dS")
                p.wait("tensor", "vC", vvw)
                for hh in range(HGRP):
                    hq = h0 + hh
                    kh = hq // 2
                    p.run("vector", lambda e, hh=hh, hq=hq, kh=kh, c=c: e.tensor_scalar(
                        out=kpt[:, hh, :], in0=Ktm[:, kh, :],
                        scalar1=wexp[:, c, 0, hq:hq + 1], scalar2=None, op0=ALU.mult))
                vvk = p.sig("vector", "vC")
                p.wait("tensor", "vC", vvk)

                first = st["first_chunk"]

                def mmpc(e, first=first):
                    for hh in range(HGRP):
                        bnk = hh // 2
                        col = (hh % 2) * 256
                        r = e.matmul(self.bank(bnk)[:, col:col + 128], Rall[:, hh, 128:256],
                                     kpt[:, hh, :], start=True, stop=True)
                        if first:
                            r = e.matmul(self.bank(bnk)[:, col + 128:col + 256], kpt[:, hh, :],
                                         Rall[:, hh, 128:256], start=True, stop=True)
                    return r
                tvp = p.run_sig("tensor", mmpc, "tC")

                def mmds(e):
                    for hh in range(HGRP):
                        bnk = 4 + hh // 4
                        r = e.matmul(self.bank(bnk)[:, (hh % 4) * 128:(hh % 4) * 128 + 128],
                                     kpt[:, hh, :], Rall[:, hh, 0:128], start=True, stop=True)
                    return r
                tvds = p.run_sig("tensor", mmds, "tC")
                p.wait("vector", "tC", tvds)
                for hh in range(HGRP):
                    hq = h0 + hh
                    bnk = hh // 2
                    col = (hh % 2) * 256
                    p.run("vector", lambda e, hh=hh, hq=hq, bnk=bnk, col=col, c=c: e.scalar_tensor_tensor(
                        out=PcT[:, hh, :], in0=self.c_identf[:, :], scalar=wexp[:, c, 3, hq:hq + 1],
                        in1=self.bank(bnk)[:, col:col + 128], op0=ALU.mult, op1=ALU.add))
                    if first:
                        p.run("vector", lambda e, hh=hh, hq=hq, h0=h0, bnk=bnk, col=col, c=c: e.scalar_tensor_tensor(
                            out=Prun[:, h0 + hh, :], in0=self.c_identf[:, :], scalar=wexp[:, c, 3, hq:hq + 1],
                            in1=self.bank(bnk)[:, col + 128:col + 256], op0=ALU.mult, op1=ALU.add))
                for hh in range(HGRP):
                    bnk = 4 + hh // 4
                    p.run("vector", lambda e, hh=hh, bnk=bnk: e.tensor_copy(
                        out=DSc[:, hh, :], in_=self.bank(bnk)[:, (hh % 4) * 128:(hh % 4) * 128 + 128]))
                vvc = p.sig("vector", "vC")
                p.wait("tensor", "vC", vvc)
                if first:
                    p.run_sig("vector", lambda e, h0=h0: e.tensor_copy(out=Brun[:, h0:h0 + HGRP, :], in_=DSc[:, :, :]), "vC")
                    p.wait("tensor", "vC", p.cnt["vC"])
                else:
                    def mmrun(e, h0=h0):
                        for hh in range(HGRP):
                            bnk = hh // 2
                            col = (hh % 2) * 256
                            e.matmul(self.bank(bnk)[:, col:col + 128], PcT[:, hh, :],
                                     Prun[:, h0 + hh, :], start=True, stop=True)
                            r = e.matmul(self.bank(bnk)[:, col + 128:col + 256], PcT[:, hh, :],
                                         Brun[:, h0 + hh, :], start=True, stop=True)
                        return r
                    tvr = p.run_sig("tensor", mmrun, "tC")
                    p.wait("vector", "tC", tvr)
                    for hh in range(HGRP):
                        bnk = hh // 2
                        col = (hh % 2) * 256
                        p.run("vector", lambda e, hh=hh, h0=h0, bnk=bnk, col=col: e.tensor_copy(
                            out=Prun[:, h0 + hh, :], in_=self.bank(bnk)[:, col:col + 128]))
                        p.run("vector", lambda e, hh=hh, h0=h0, bnk=bnk, col=col: e.tensor_add(
                            Brun[:, h0 + hh, :], self.bank(bnk)[:, col + 128:col + 256], DSc[:, hh, :]))
                    p.sig("vector", "vC")
                    p.wait("tensor", "vC", p.cnt["vC"])
            if c == 0:
                st["first_chunk"] = False

        emit_a1_half(0)
        for c in range(4):
            emit_a2_chunk(c)
        emit_a1_half(1)
        for c in range(4, NCH):
            emit_a2_chunk(c)

        for g in range(NGRP):
            h0 = g * HGRP
            p.wait("tensor", "vC", p.cnt["vC"])

            def mmpt(e, h0=h0):
                for hh in range(HGRP):
                    bnk = hh // 2
                    col = (hh % 2) * 256
                    r = e.matmul(self.bank(bnk)[:, col:col + 128], Prun[:, h0 + hh, :],
                                 self.c_identf[:, :], start=True, stop=True)
                return r
            tvt = p.run_sig("tensor", mmpt, "tC")
            p.wait("vector", "tC", tvt)
            for hh in range(HGRP):
                bnk = hh // 2
                col = (hh % 2) * 256
                p.run("vector", lambda e, hh=hh, bnk=bnk, col=col: e.tensor_copy(
                    out=PcT[:, hh, :], in_=self.bank(bnk)[:, col:col + 128]))
            vvt2 = p.sig("vector", "vC")
            p.wait("sync", "vC", vvt2)
            p.dma("sync", d["ag2_in"][0, h0:h0 + HGRP].rearrange("m p t -> p m t"), PcT[:, :, :], "dS")
            p.wait("tensor", "vC", vvt2)
        p.wait("sync", "vC", p.cnt["vC"])
        p.dma("sync", d["ag2_in"][1].rearrange("m p t -> p m t"), Brun[:, :, :], "dS")
        p.wait("gpsimd", "dS", p.cnt["dS"])
        p.run_sig("gpsimd", lambda e: e.collective_compute(
            "AllGather", ALU.bypass, replica_groups=[list(range(NCORES))],
            ins=[d["ag2_in"].opt()], outs=[d["ag2_out"].opt()]), "cc")
        p.barrier()
        for cm in reversed(cms1):
            cm.__exit__(None, None, None)
        cms2 = []
        cms = cms2
        Sst = al("Sst", [128, NH, 128], F32)
        Sb16 = al("Sb16", [128, NH, 128], BF16)
        cms2a = []
        cms = cms2a
        runS = al("runS", [128, NH, 128], F32)
        pbufP = al("pbufP", [128, NH, 128], F32)
        pbufB = al("pbufB", [128, NH, 128], F32)
        cms = cms2
        p.run("vector", lambda e: e.memset(Sst[:, :, :], 0.0))
        p.run_sig("vector", lambda e: e.memset(runS[:, :, :], 0.0), "vC")
        p.wait("sync", "cc", p.cnt["cc"])
        for dd in range(NCORES):
            vp = p.dma("sync", pbufP[:, :, :], d["ag2_out"][dd, 0].rearrange("m p t -> p m t"), "dS")
            vb = p.dma("sync", pbufB[:, :, :], d["ag2_out"][dd, 1].rearrange("m p t -> p m t"), "dS")
            if dd > 0:
                p.run_sig("vector", lambda e, dd=dd: e.scalar_tensor_tensor(
                    out=Sst[:, :, :], in0=runS[:, :, :], scalar=self.c_oh[:, dd:dd + 1],
                    in1=Sst[:, :, :], op0=ALU.mult, op1=ALU.add), "vC")
            p.wait("vector", "dS", vb)
            if dd == 0:
                p.run_sig("vector", lambda e: e.tensor_copy(out=runS[:, :, :], in_=pbufB[:, :, :]), "vC")
                p.wait("sync", "vC", p.cnt["vC"])
                p.wait("tensor", "vC", p.cnt["vC"])
            elif dd < NCORES - 1:
                p.wait("tensor", "dS", vb)
                p.wait("tensor", "vC", p.cnt["vC"])

                def mmpre(e):
                    for hq in range(NH):
                        bnk = hq % 8
                        col = ((hq // 8) % 4) * 128
                        r = e.matmul(self.bank(bnk)[:, col:col + 128], pbufP[:, hq, :],
                                     runS[:, hq, :], start=True, stop=True)
                    return r
                tvpre = p.run_sig("tensor", mmpre, "tC")
                p.wait("vector", "tC", tvpre)
                for hq in range(NH):
                    bnk = hq % 8
                    col = ((hq // 8) % 4) * 128
                    p.run("vector", lambda e, hq=hq, bnk=bnk, col=col: e.tensor_add(
                        runS[:, hq, :], self.bank(bnk)[:, col:col + 128], pbufB[:, hq, :]))
                vpr = p.sig("vector", "vC")
                p.wait("sync", "vC", vpr)
                p.wait("tensor", "vC", vpr)
        p.run_sig("vector", lambda e: e.tensor_copy(out=Sb16[:, :, :], in_=Sst[:, :, :]), "vC")
        self.dump("S_in", Sst[:, :, :], [128, NH, 128])
        p.barrier()
        for cm in reversed(cms2a):
            cm.__exit__(None, None, None)

        self.c_Ecol = al("c_Ecol", [128, NH, 128], BF16)
        self.c_Erow = al("c_Erow", [128, NH, 128], BF16)
        p.dma("sync", self.c_Ecol[:, :, :], d["E_col"].rearrange("m p t -> p m t"), "dS")
        p.dma("sync", self.c_Erow[:, :, :], d["E_row"].rearrange("m p t -> p m t"), "dS")
        mtb = al("mtb", [128, NH, 128], BF16)
        uab = al("uab", [128, NH, 128], BF16)
        mwb = al("mwb", [128, NH, 128], BF16)
        wbcb = al("wbcb", [128, NH, 128], BF16)
        qrel = al("qrel", [128, NKH, 128], BF16)
        ktmb = al("ktmb", [128, NKH, 128], BF16)
        zrel = al("zrel", [128, NH, 128], BF16)
        szt = al("szt", [128, NH, 128], BF16)
        qwt = al("qwt", [128, NH, 128], BF16)
        ub = al("ub", [128, NH, 128], BF16)
        xgt = al("xgt", [128, NH, 128], BF16)
        sqg = al("sqg", [128, NH, 128], BF16)
        rmst = al("rmst", [128, 128], F32)
        rinvb = al("rinvb", [128, 128], BF16)
        yt = al("yt", [128, NH, 128], BF16)
        kptB = al("kpt2", [128, HGRP, 128], BF16)
        p.wait("scalar", "dS", p.cnt["dS"])
        p.wait("tensor", "dS", p.cnt["dS"])
        for c in range(NCH):
            gsl = slice(c * C, (c + 1) * C)
            vmt = p.dma("scalar", mtb[:, :, :], d["mt_sp"][c].rearrange("m p t -> p m t"), "dA")
            vua = p.dma("scalar", uab[:, :, :], d["ua_sp"][c].rearrange("m p t -> p m t"), "dA")
            vmw = p.dma("scalar", mwb[:, :, :], d["mw_sp"][c].rearrange("m p t -> p m t"), "dA")
            vwb = p.dma("scalar", wbcb[:, :, :], d["wbc_sp"][c].rearrange("m p t -> p m t"), "dA")
            vq = p.dma("scalar", qrel[:, :, :], d["q_sp"][:, :, gsl].rearrange("m p t -> p m t"), "dA")
            vkt = p.dma("scalar", ktmb[:, :, :], d["ktm_sp"][c].rearrange("m p t -> p m t"), "dA")
            vz = p.dma("scalar", zrel[:, :, :],
                       d["z_sp"][c].rearrange("m p t -> p m t"), "dA")
            p.wait("scalar", "dA", vz)
            svz = p.run_sig("scalar", lambda e: e.activation(out=szt[:, :, :], in_=zrel[:, :, :], func=AF.Silu), "sC")
            p.wait("vector", "dA", vq)
            for hq in range(NH):
                p.run("vector", lambda e, hq=hq: e.tensor_mul(
                    qwt[:, hq, :], qrel[:, hq // 2, :], wbcb[:, hq, :]))
            vqw = p.sig("vector", "vC")
            p.wait("tensor", "dA", vq)
            for g in range(NGRP):
                h0 = g * HGRP
                p.wait("tensor", "vC", p.cnt["vC"])

                def mmu(e, h0=h0):
                    for hh in range(HGRP):
                        bnk = hh // 4
                        col = (hh % 4) * 128
                        e.matmul(self.bank(bnk)[:, col:col + 128], self.c_identb[:, :],
                                 uab[:, h0 + hh, :], start=True, stop=False)
                        r = e.matmul(self.bank(bnk)[:, col:col + 128], mwb[:, h0 + hh, :],
                                     Sb16[:, h0 + hh, :], start=False, stop=True)
                    return r
                tvu = p.run_sig("tensor", mmu, "tC")
                p.wait("scalar", "tC", tvu)
                for bnk in range(2):
                    p.run("scalar", lambda e, h0=h0, bnk=bnk: e.copy(
                        out=ub[:, h0 + bnk * 4:h0 + bnk * 4 + 4, :], in_=self.bank(bnk)))
                svu = p.sig("scalar", "sC")
                p.wait("tensor", "sC", svu)
                def mmo(e, h0=h0):
                    for hh in range(HGRP):
                        bnk = 2 + hh // 4
                        col = (hh % 4) * 128
                        e.matmul(self.bank(bnk)[:, col:col + 128], Sb16[:, h0 + hh, :],
                                 qwt[:, h0 + hh, :], start=True, stop=False)
                        r = e.matmul(self.bank(bnk)[:, col:col + 128], ub[:, h0 + hh, :],
                                     mtb[:, h0 + hh, :], start=False, stop=True)
                    return r
                tvo = p.run_sig("tensor", mmo, "tC")
                p.wait("vector", "tC", tvo)
                p.wait("vector", "sC", svz)
                for bnk in range(2):
                    p.run("vector", lambda e, h0=h0, bnk=bnk: e.tensor_mul(
                        xgt[:, h0 + bnk * 4:h0 + bnk * 4 + 4, :], self.bank(2 + bnk),
                        szt[:, h0 + bnk * 4:h0 + bnk * 4 + 4, :]))
                vvo = p.sig("vector", "vC")
                p.wait("tensor", "vC", vvo)
                for hh in range(HGRP):
                    hq = h0 + hh
                    p.run("vector", lambda e, hh=hh, hq=hq, c=c: e.tensor_scalar(
                        out=kptB[:, hh, :], in0=ktmb[:, hq // 2, :],
                        scalar1=wexp[:, c, 0, hq:hq + 1], scalar2=None, op0=ALU.mult))
                vvk2 = p.sig("vector", "vC")
                p.wait("tensor", "vC", vvk2)

                def mms(e, h0=h0):
                    for hh in range(HGRP):
                        bnk = 4 + hh // 4
                        col = (hh % 4) * 128
                        r = e.matmul(self.bank(bnk)[:, col:col + 128], kptB[:, hh, :],
                                     ub[:, h0 + hh, :], start=True, stop=True)
                    return r
                tvs = p.run_sig("tensor", mms, "tC")
                p.wait("vector", "tC", tvs)
                for hh in range(HGRP):
                    hq = h0 + hh
                    bnk = 4 + hh // 4
                    col = (hh % 4) * 128
                    p.run("vector", lambda e, hh=hh, hq=hq, bnk=bnk, col=col, c=c: e.scalar_tensor_tensor(
                        out=Sst[:, hq, :], in0=Sst[:, hq, :], scalar=wexp[:, c, 3, hq:hq + 1],
                        in1=self.bank(bnk)[:, col:col + 128], op0=ALU.mult, op1=ALU.add))
                vvs2 = p.sig("vector", "vC")
                p.wait("tensor", "vC", vvs2)
            self.v_gap()
            p.run_sig("vector", lambda e: e.tensor_copy(out=Sb16[:, :, :], in_=Sst[:, :, :]), "vC")
            p.wait("scalar", "vC", p.cnt["vC"])
            svq = p.run_sig("scalar", lambda e: e.activation(out=sqg[:, :, :], in_=xgt[:, :, :], func=AF.Square), "sC")
            p.wait("tensor", "sC", svq)

            def mmssq(e):
                for hq in range(NH):
                    r = e.matmul(self.bank(6)[:, 0:128], self.c_Ecol[:, hq, :], sqg[:, hq, :],
                                 start=(hq == 0), stop=(hq == NH - 1))
                return r
            tvq2 = p.run_sig("tensor", mmssq, "tC")
            p.wait("vector", "tC", tvq2)
            p.run_sig("vector", lambda e: e.tensor_scalar(
                out=rmst[:, :], in0=self.bank(6)[:, 0:128], scalar1=1.0 / 128,
                scalar2=self.c_cols[:, 4:5], op0=ALU.mult, op1=ALU.add), "vC")
            p.wait("scalar", "vC", p.cnt["vC"])
            p.run_sig("scalar", lambda e: e.activation(out=rmst[:, :], in_=rmst[:, :], func=AF.Sqrt), "sC")
            p.wait("vector", "sC", p.cnt["sC"])
            p.run("vector", lambda e: e.reciprocal(out=rmst[:, :], in_=rmst[:, :]))
            self.v_gap()
            p.run_sig("vector", lambda e: e.tensor_copy(out=rinvb[:, :], in_=rmst[:, :]), "vC")
            p.wait("tensor", "vC", p.cnt["vC"])
            for g in range(NGRP):
                h0 = g * HGRP

                def mmrb(e, h0=h0):
                    for hh in range(HGRP):
                        bnk = 6 + hh // 4
                        col = (hh % 4) * 128
                        r = e.matmul(self.bank(bnk)[:, col:col + 128], self.c_Erow[:, h0 + hh, :],
                                     rinvb[:, :], start=True, stop=True)
                    return r
                tvb = p.run_sig("tensor", mmrb, "tC")
                p.wait("vector", "tC", tvb)
                if h0 == 0:
                    p.wait("vector", "dS", st.get("y_dma", 0))
                for bnk in range(2):
                    p.run("vector", lambda e, h0=h0, bnk=bnk: e.tensor_mul(
                        yt[:, h0 + bnk * 4:h0 + bnk * 4 + 4, :], self.bank(6 + bnk),
                        xgt[:, h0 + bnk * 4:h0 + bnk * 4 + 4, :]))
                vvy = p.sig("vector", "vC")
                p.wait("tensor", "vC", vvy)
            p.wait("scalar", "vC", p.cnt["vC"])
            p.wait("scalar", "dS", st.get("y_dma", 0))
            svy = p.run_sig("scalar", lambda e: e.activation(
                out=yt[:, :, :], in_=yt[:, :, :], func=AF.Copy, scale=self.c_normw[:, :]), "sC")
            p.wait("sync", "sC", svy)
            st["y_dma"] = p.dma("sync", d["y_sp"][:, :, c * C:(c + 1) * C].rearrange("m p t -> p m t"), yt[:, :, :], "dS")
        p.barrier()
        for cm in reversed(cms2):
            cm.__exit__(None, None, None)
        for cm in reversed(cms_keep):
            cm.__exit__(None, None, None)

    def rebuild_x3(self):
        p = self.p
        d = self.d
        self.alloc_x()
        X = self.X
        cmx = []

        def al2(n_, s_, t_):
            cm = self.nc.sbuf_tensor(n_, s_, t_)
            h = cm.__enter__()
            cmx.append(cm)
            return h
        wob = al2("wob", [128, 2, 8, DC, 128], BF16)
        yrl = al2("yrl", [128, 2, 8, TH], BF16)
        p.wait("sync", "dS", p.cnt["dS"])
        vx = p.dma("sync", X[:, :, :], d["x2_sp"].rearrange("c p t -> p c t"), "dIn2")
        p.wait("vector", "dIn2", vx)
        evA = {}
        slab_done = {}
        ydma_done = {}
        for kg in range(4):
            ws = kg % 2
            if (kg - 2) in slab_done:
                p.wait("gpsimd", "tC", slab_done[kg - 2])
            vw = p.dma("gpsimd", wob[:, ws, :, :, :], d["wout_t"][:, kg * 8:kg * 8 + 8, :, :], "dW")
            for h in range(2):
                hsl = slice(h * TH, (h + 1) * TH)
                yb = h % 2
                if (kg - 1, h) in ydma_done:
                    p.wait("scalar", "tC", ydma_done[(kg - 1, h)])
                vy = p.dma("scalar", yrl[:, yb, :, :],
                           d["y_sp"][kg * 8:kg * 8 + 8, :, hsl].rearrange("k p t -> p k t"), "dA")
                p.wait("tensor", "dW", vw)
                p.wait("tensor", "dA", vy)
                tvop = None
                for ocg in range(2):
                    for oc in range(ocg * 8, ocg * 8 + 8):
                        bnk = oc % 8
                        if kg > 0:
                            p.wait("tensor", "vC", evA[(h, oc)])

                        def mmop(e, oc=oc, bnk=bnk, yb=yb, ws=ws):
                            for kk in range(8):
                                r = e.matmul(self.bank(bnk), wob[:, ws, kk, oc, :], yrl[:, yb, kk, :],
                                             start=(kk == 0), stop=(kk == 7))
                            return r
                        tvop = p.run_sig("tensor", mmop, "tC")
                        p.wait("vector", "tC", tvop)
                        evA[(h, oc)] = p.run_sig("vector", lambda e, oc=oc, bnk=bnk, hsl=hsl: e.tensor_add(
                            X[:, oc, hsl], X[:, oc, hsl], self.bank(bnk)), "vC")
                ydma_done[(kg, h)] = tvop
            slab_done[kg] = tvop
        for cm in reversed(cmx):
            cm.__exit__(None, None, None)
        p.barrier()

    def finale(self):
        p = self.p
        p.wait("sync", "dOut", p.cnt["dOut"]) if "dOut" in p.cnt else None
        for s in ("dIn", "dW", "dW2", "dS", "dA"):
            if s in p.cnt:
                p.wait("sync", s, p.cnt[s])


def build(dbg=(), upto="all", races=True):
    nc = bass.Bass("TRN2", target_bir_lowering=False, debug=False, detect_race_conditions=races)
    with nc.Block() as block:
        k = Kern(nc, dbg=dbg)
        k.alloc_x()
        k.phase0()
        k.layernorm(k.X, k.lnb, "1")
        if upto == "ln":
            k.dump("x2", k.X[:, :, :], [128, DC, T])
            k.finale()
            k.p.emit(block)
            k.close()
            return nc
        k.ffn(k.X, call=0, only_masks=(upto == "masks"))
        k.dump("x2", k.X[:, :, :], [128, DC, T])
        if upto in ("ffn1", "masks"):
            k.finale()
            k.p.emit(block)
            k.close()
            return nc
        k.layernorm(k.X, k.lnb, "2")
        k.attn()
        k.rebuild_x3()
        if upto == "attn":
            k.dump("x3", k.X[:, :, :], [128, DC, T])
            k.finale()
            k.p.emit(block)
            k.close()
            return nc
        k.layernorm(k.X, k.lnb, "3")
        k.ffn(k.X, call=1, final=True)
        k.dump("x4", k.X[:, :, :], [128, DC, T])
        k.finale()
        k.p.emit(block)
        k.close()
    return nc


_CACHE = {}


def make_in_maps(sh, cn, per_core):
    import ml_dtypes
    in_maps = []
    for c in range(NCORES):
        m = dict(sh)
        m["identf"] = cn["identf"]
        m["onesf"] = cn["onesf"]
        m["identb"] = cn["identf"].astype(ml_dtypes.bfloat16)
        m["negidentb"] = (-cn["identf"]).astype(ml_dtypes.bfloat16)
        m["onesb"] = cn["onesf"].astype(ml_dtypes.bfloat16)
        m["negmaskU_T"] = cn["negmaskU_T"]
        m["negmaskI_T"] = cn["negmaskI_T"]
        m["negmaskL_T"] = cn["negmaskL_T"]
        m["E_col"] = cn["E_col"].astype(ml_dtypes.bfloat16)
        m["E_row"] = cn["E_row"].astype(ml_dtypes.bfloat16)
        m["E8_row"] = cn["E8_row"]
        m["ccols"] = cn["ccols"]
        m.update(per_core[c])
        in_maps.append(m)
    return in_maps


def kernel(**inputs):
    sh, cn, per_core = host_prep(inputs)
    if "full" not in _CACHE:
        _CACHE["full"] = build()
    nc = _CACHE["full"]
    in_maps = make_in_maps(sh, cn, per_core)
    res = run_bass_kernel_spmd(nc, in_maps, core_ids=list(range(NCORES)))
    out = np.empty((1, NCORES * T, D), np.float32)
    for c in range(NCORES):
        out[0, c * T:(c + 1) * T, :] = res.results[c]["out_fm"].T
    return out
